# version 19 (final)
# baseline (speedup 1.0000x reference)
import numpy as np
import concourse.bass as bass
import concourse.mybir as mybir
from concourse.bass_utils import run_bass_kernel_spmd

F32 = mybir.dt.float32
BF16 = mybir.dt.bfloat16
I32 = mybir.dt.int32
AF = mybir.ActivationFunctionType
ALU = mybir.AluOpType
AX = mybir.AxisListType

S = 2048
D = 1024
NT = S // 128
ND = D // 128
LNR = float(np.log(1.0 / 8.0))
NRM = float(64.0 ** -0.25)

ENGS = ("tensor", "vector", "scalar", "gpsimd", "sync")
NDMASLOT = {"sync": 6, "gpsimd": 10, "scalar": 4}



class Op:
    __slots__ = ("eng", "idx", "builder", "deps", "dma", "slot", "slotval", "sig",
                 "seq", "dma_waits")

    def __init__(self, eng, idx, builder, deps, dma, slot, slotval):
        self.eng, self.idx, self.builder = eng, idx, builder
        self.deps, self.dma, self.slot, self.slotval = deps, dma, slot, slotval
        self.sig = dma
        self.seq = None
        self.dma_waits = {}


class Buf:
    __slots__ = ("ws", "rs", "name")

    def __init__(self, name=""):
        self.ws, self.rs, self.name = [], [], name


class Prog:
    def __init__(self, nc):
        self.nc = nc
        self.ops = []
        self.streams = {e: [] for e in ENGS}
        self.dma_next = {e: 0 for e in NDMASLOT}
        self.dma_cnt = {}
        self.fence_ops = []
        self.fence_dma = {}

    def fence(self):
        self.fence_ops = [s[-1] for s in self.streams.values() if s]
        self.fence_dma = {slot: 16 * cnt for slot, cnt in self.dma_cnt.items()}

    def do(self, eng, builder, reads=(), writes=(), deps=(), dma=False):
        dep_ops = {}
        for b in reads:
            for w in b.ws:
                dep_ops[w.idx] = w
        for b in writes:
            for w in b.ws:
                dep_ops[w.idx] = w
            for r in b.rs:
                dep_ops[r.idx] = r
        for d in deps:
            if d is not None:
                dep_ops[d.idx] = d
        for f in self.fence_ops:
            dep_ops[f.idx] = f
        dma_waits = dict(self.fence_dma)
        for d in dep_ops.values():
            if d.dma:
                dma_waits[d.slot] = 16 * self.dma_cnt[d.slot]
        slot = slotval = None
        if dma:
            sl = self.dma_next[eng]
            self.dma_next[eng] = (sl + 1) % NDMASLOT[eng]
            slot = (eng, sl)
            prior = 16 * self.dma_cnt.get(slot, 0)
            if prior > dma_waits.get(slot, 0):
                dma_waits[slot] = prior
            self.dma_cnt[slot] = self.dma_cnt.get(slot, 0) + 1
            slotval = 16 * self.dma_cnt[slot]
        op = Op(eng, len(self.ops), builder,
                [dep_ops[i] for i in sorted(dep_ops)], dma, slot, slotval)
        op.dma_waits = dma_waits
        self.ops.append(op)
        self.streams[eng].append(op)
        for b in reads:
            b.rs.append(op)
        for b in writes:
            b.ws = [op]
            b.rs = []
        for d in op.deps:
            if d.dma:
                continue
            if d.eng == "tensor" and eng == "tensor":
                continue
            d.sig = True
        return op

    def finalize_and_emit(self, final_dma_ops):
        nc = self.nc
        cnt = {e: 0 for e in ENGS}
        for op in self.ops:
            if op.sig and not op.dma:
                cnt[op.eng] += 1
                op.seq = cnt[op.eng]
        csem = {e: nc.alloc_semaphore(name=f"c_{e}") for e in ENGS}
        dsem = {}
        for e, n in NDMASLOT.items():
            for sl in range(n):
                dsem[(e, sl)] = nc.alloc_semaphore(name=f"d_{e}{sl}")
        streams = self.streams

        def emit_stream(eng_name):
            def run(engine):
                wm = {}
                for op in streams[eng_name]:
                    for slot, val in sorted(op.dma_waits.items()):
                        if wm.get(slot, 0) < val:
                            wm[slot] = val
                            engine.wait_ge(dsem[slot], val)
                    for d in op.deps:
                        if d.dma:
                            continue
                        if d.eng == "tensor" and eng_name == "tensor":
                            continue
                        key, val, sem = d.eng, d.seq, csem[d.eng]
                        if wm.get(key, 0) < val:
                            wm[key] = val
                            engine.wait_ge(sem, val)
                    inst = op.builder(engine)
                    if op.dma:
                        inst.then_inc(dsem[op.slot], 16)
                    elif op.sig:
                        inst.then_inc(csem[op.eng], 1)
                if eng_name == "sync":
                    fin = {}
                    for d in final_dma_ops:
                        fin[d.slot] = max(fin.get(d.slot, 0), d.slotval)
                    for slot, val in sorted(fin.items()):
                        engine.wait_ge(dsem[slot], val)
            return run

        with nc.Block() as block:
            block.tensor(emit_stream("tensor"))
            block.vector(emit_stream("vector"))
            block.scalar(emit_stream("scalar"))
            block.gpsimd(emit_stream("gpsimd"))
            block.sync(emit_stream("sync"))


class Arena:
    def __init__(self, nc, name, nbytes):
        self.t = nc.alloc_sbuf_tensor(name, [128, nbytes], mybir.dt.uint8)
        self.nbytes = nbytes
        self.off = 0
        self.name = name

    def take(self, dtype, rows, cols, align=32):
        esz = mybir.dt.size(dtype)
        off = (self.off + align - 1) // align * align
        need = cols * esz
        assert off + need <= self.nbytes, \
            f"arena {self.name} overflow: {off}+{need} > {self.nbytes}"
        self.off = off + need
        return self.at(off, dtype, rows, cols)

    def at(self, off, dtype, rows, cols):
        esz = mybir.dt.size(dtype)
        assert off + cols * esz <= self.nbytes
        v = self.t.ap()[:, off:off + cols * esz].bitcast(dtype)
        if rows != 128:
            v = v[0:rows, :]
        return v

    def reset(self, off=0):
        self.off = off


class Ctx:
    pass


def fma(fmbuf, c, t0, tn):
    return fmbuf[:, c * S + t0: c * S + t0 + tn]


def build_context(nc, p):
    cx = Ctx()
    cx.nc, cx.p = nc, p
    cx.X = Arena(nc, "AX", 65536)
    cx.CONST = Arena(nc, "ACONST", 2880)
    cx.BIG1 = Arena(nc, "ABIG1", 32768)
    cx.BIG2 = Arena(nc, "ABIG2", 32768)
    cx.STRIPE = Arena(nc, "ASTRIPE", 20480)
    cx.WT = Arena(nc, "AWT", 40960)
    cx.SCR = Arena(nc, "ASCR", 9216)
    cx.SCR2 = Arena(nc, "ASCR2", 8192)

    cx.xt = [cx.X.take(F32, 128, 1024) for _ in range(NT)]
    cx.bx = [Buf(f"x{t}") for t in range(NT)]

    cx.psum = [nc.alloc_psum_tensor(f"pb{i}", [128, 512], F32) for i in range(6)]
    cx.psumT = [nc.alloc_psum_tensor(f"pt{i}", [128, 1024], BF16) for i in range(2)]
    cx.b_psum = [Buf(f"pb{i}") for i in range(6)]
    cx.b_psumT = [Buf(f"pt{i}") for i in range(2)]

    cx.ident_bf = cx.CONST.take(BF16, 128, 128)
    cx.ident_f = cx.CONST.take(F32, 128, 128)
    cx.ones_row = cx.CONST.take(BF16, 1, 512)
    cx.ones_col = cx.CONST.take(BF16, 128, 1)
    cx.onesf_row = cx.CONST.take(F32, 1, 128)
    cx.onesf2 = cx.CONST.take(F32, 2, 128)
    cx.b_ident_bf, cx.b_ident_f = Buf("idb"), Buf("idf")
    cx.b_ones_row, cx.b_ones_col, cx.b_onesf = Buf("or"), Buf("oc"), Buf("ofr")
    ib, idf = cx.ident_bf, cx.ident_f
    p.do("gpsimd", lambda g: g.memset(ib, 1.0), writes=[cx.b_ident_bf])
    p.do("gpsimd", lambda g: g.affine_select(ib, ib, [[1, 128]], ALU.is_equal, 0.0,
                                             base=0, channel_multiplier=-1),
         writes=[cx.b_ident_bf])
    p.do("gpsimd", lambda g: g.memset(idf, 1.0), writes=[cx.b_ident_f])
    p.do("gpsimd", lambda g: g.affine_select(idf, idf, [[1, 128]], ALU.is_equal, 0.0,
                                             base=0, channel_multiplier=-1),
         writes=[cx.b_ident_f])
    p.do("gpsimd", lambda g: g.memset(cx.ones_row, 1.0), writes=[cx.b_ones_row])
    p.do("gpsimd", lambda g: g.memset(cx.ones_col, 1.0), writes=[cx.b_ones_col])
    p.do("gpsimd", lambda g: g.memset(cx.onesf_row, 1.0), writes=[cx.b_onesf])
    p.do("gpsimd", lambda g: g.memset(cx.onesf2, 1.0), writes=[cx.b_onesf])
    cx.b_lnst, cx.b_lnd = Buf("lnst"), Buf("lnd")
    cx.ccn = 0
    return cx


def load_x(cx, xin):
    p = cx.p
    for tc in range(NT):
        p.do("sync", lambda s, tc=tc: s.dma_start(
            out=cx.xt[tc], in_=xin[tc * 128:(tc + 1) * 128, :]),
            writes=[cx.bx[tc]], dma=True)
        ln_stats_tile(cx, tc)


def store_x(cx, xout):
    p = cx.p
    return [p.do("sync", lambda s, tc=tc: s.dma_start(
        out=xout[tc * 128:(tc + 1) * 128, :], in_=cx.xt[tc]),
        reads=[cx.bx[tc]], dma=True) for tc in range(NT)]


def transpose_bf(cx, src_ap, dst_ap, b_src, b_dst, pb):
    p = cx.p
    pv = cx.psumT[pb].ap()[:, 0:128]
    p.do("tensor", lambda t, src_ap=src_ap, pv=pv: t.transpose(pv, src_ap, cx.ident_bf),
         reads=[b_src, cx.b_ident_bf], writes=[cx.b_psumT[pb]])
    p.do("vector", lambda v, dst_ap=dst_ap, pv=pv: v.tensor_copy(out=dst_ap, in_=pv),
         reads=[cx.b_psumT[pb]], writes=[b_dst])


def transposeN_bf(cx, srcs, b_srcs, dst_ap, b_dst, pb):
    p = cx.p
    n = len(srcs)
    bank = cx.psumT[pb]
    for i, s in enumerate(srcs):
        pv = bank.ap()[:, i * 128:(i + 1) * 128]
        p.do("tensor", lambda t, s=s, pv=pv: t.transpose(pv, s, cx.ident_bf),
             reads=[b_srcs[i], cx.b_ident_bf], writes=[cx.b_psumT[pb]])
    pvall = bank.ap()[:, 0:n * 128]
    p.do("vector", lambda v, dst_ap=dst_ap, pvall=pvall: v.tensor_copy(
        out=dst_ap, in_=pvall),
        reads=[cx.b_psumT[pb]], writes=[b_dst])



def ln_stats_tile(cx, tc):
    p = cx.p
    A = cx.SCR2
    ssum = A.at(0, F32, 128, NT)
    ssq = A.at(64, F32, 128, NT)
    lnd = A.at(320, BF16, 128, 1024)
    xa = cx.xt[tc]
    p.do("vector", lambda v, xa=xa, tc=tc: v.reduce_sum(
        out=ssum[:, tc:tc + 1], in_=xa, axis=AX.X), reads=[cx.bx[tc]],
        writes=[cx.b_lnst])
    p.do("scalar", lambda s, xa=xa, tc=tc: s.activation(
        out=lnd, in_=xa, func=AF.Square, accum_out=ssq[:, tc:tc + 1]),
        reads=[cx.bx[tc]], writes=[cx.b_lnd, cx.b_lnst])


def layernorm(cx, out_fm, b_out, per_tile_cb=None, stats_out=None, pre_stats=False):
    p = cx.p
    A = cx.SCR2
    ssum = A.at(0, F32, 128, NT)
    ssq = A.at(64, F32, 128, NT)
    smean = A.at(128, F32, 128, NT)
    srstd = A.at(192, F32, 128, NT)
    svar = A.at(256, F32, 128, NT)
    lnd = A.at(320, BF16, 128, 1024)
    b_st, b_lnd = cx.b_lnst, cx.b_lnd
    if not pre_stats:
        for tc in range(NT):
            ln_stats_tile(cx, tc)
    p.do("vector", lambda v: v.tensor_scalar(out=smean, in0=ssum, scalar1=1.0 / D,
                                             scalar2=None, op0=ALU.mult),
         reads=[b_st], writes=[b_st])
    p.do("vector", lambda v: v.tensor_mul(out=svar, in0=smean, in1=smean),
         reads=[b_st], writes=[b_st])
    p.do("vector", lambda v: v.scalar_tensor_tensor(out=svar, in0=ssq, scalar=1.0 / D,
                                                    in1=svar, op0=ALU.mult, op1=ALU.subtract),
         reads=[b_st], writes=[b_st])
    p.do("vector", lambda v: v.tensor_scalar(out=svar, in0=svar, scalar1=1e-5,
                                             scalar2=None, op0=ALU.add),
         reads=[b_st], writes=[b_st])
    p.do("scalar", lambda s: s.activation(out=svar, in_=svar, func=AF.Sqrt),
         reads=[b_st], writes=[b_st])
    p.do("vector", lambda v: v.reciprocal(out=srstd, in_=svar),
         reads=[b_st], writes=[b_st])
    if stats_out is not None:
        stats_out[0], stats_out[1], stats_out[2] = smean, srstd, b_st
    for tc in range(NT):
        xa = cx.xt[tc]
        p.do("vector", lambda v, xa=xa, tc=tc: v.tensor_scalar(
            out=lnd, in0=xa, scalar1=smean[:, tc:tc + 1], scalar2=srstd[:, tc:tc + 1],
            op0=ALU.subtract, op1=ALU.mult), reads=[cx.bx[tc], b_st], writes=[b_lnd])
        dstv = out_fm[:, tc * 128: (ND - 1) * S + tc * 128 + 128].rearrange(
            "p (c i) -> p c i", c=ND) if False else None
        import concourse.bass as _b
        dv = out_fm.rearrange("p (c i) -> p c i", c=ND)[:, :, tc * 128: tc * 128 + 128]
        transposeN_bf(cx, [lnd[:, c * 128:(c + 1) * 128] for c in range(ND)],
                      [b_lnd] * ND, dv, b_out, pb=tc % 2)
        if per_tile_cb is not None:
            per_tile_cb(tc, lnd, b_lnd)



def build_l0_attn(cx, H):
    nc, p = cx.nc, cx.p
    p.fence()
    for a in (cx.BIG1, cx.BIG2, cx.STRIPE, cx.WT, cx.SCR):
        a.reset()

    ln1 = cx.BIG1.take(BF16, 128, ND * S); b_ln1 = Buf("ln1")
    att = cx.BIG2.take(BF16, 128, ND * S); b_att = Buf("att")
    qf2 = [cx.STRIPE.take(BF16, 128, S), cx.WT.take(BF16, 128, S)]
    b_qf2 = [Buf("qf0"), Buf("qf1")]
    kf2 = [cx.STRIPE.take(BF16, 128, S), cx.WT.take(BF16, 128, S)]
    b_kf2 = [Buf("kf0"), Buf("kf1")]
    vt2 = [cx.STRIPE.take(BF16, 128, S), cx.WT.take(BF16, 128, S)]
    b_vt2 = [Buf("vt0"), Buf("vt1")]
    qp = cx.STRIPE.take(BF16, 128, S); b_qp = Buf("qp")
    kp = cx.STRIPE.take(BF16, 128, S); b_kp = Buf("kp")

    wsl = [cx.WT.take(BF16, 128, ND * 128) for _ in range(6)]
    b_wsl = [Buf(f"w{i}") for i in range(6)]
    wo = cx.WT.take(BF16, 128, ND * D); b_wo = Buf("wo")

    A = cx.SCR
    bqc = A.take(F32, 128, ND); b_bqc = Buf("bqc")
    bkc = A.take(F32, 128, ND); b_bkc = Buf("bkc")
    bvc = A.take(F32, 128, ND); b_bvc = Buf("bvc")
    bq_s = A.take(F32, 128, ND); b_bqs = Buf("bqs")
    bk_s = A.take(F32, 128, ND); b_bks = Buf("bks")
    bo_row = A.take(BF16, 1, D); b_bor = Buf("bor")
    projbd = A.take(BF16, 128, 8 * 128); b_projbd = Buf("projbd")
    halfbd = A.take(BF16, 128, 2); b_hbd = Buf("halfbd")
    dq_tm = A.take(F32, 128, 2 * NT); b_dqt = Buf("dqt")
    dk_tm = A.take(F32, 128, 2 * NT); b_dkt = Buf("dkt")
    mxk = A.take(F32, 128, 2 * NT); b_mxk = Buf("mxk")
    stq_all = A.take(F32, 128, 2 * NT); b_stq = Buf("stq_all")
    biasq = A.take(F32, 128, 2 * NT); b_biasq = Buf("biasq")
    biask = A.take(F32, 128, 2 * NT); b_biask = Buf("biask")
    nrm_all = A.take(F32, 128, 2 * NT); b_nrm = Buf("nrm_all")
    mxr = A.take(F32, 128, 2); b_mxr = Buf("mxr")
    stabkb = A.take(F32, 128, 2); b_stabkb = Buf("stabkb")
    kvblk = A.take(BF16, 128, 132); b_kvblk = Buf("kvblk")
    ksrow = A.take(F32, 1, 128); b_ksrow = Buf("ksrow")
    ksrb = A.take(F32, 128, 128); b_ksrb = Buf("ksrb")
    ksr2 = A.take(F32, 2, 128); b_ksr2 = Buf("ksr2")
    q2 = cx.SCR2.at(4096, BF16, 128, 2048); b_q2 = Buf("q2")
    qsc = A.take(BF16, 128, 1024); b_qsc = Buf("qsc")

    def th_bank_view(ap_, g):
        return ap_.rearrange("p (q g h) -> p g q h", q=4, g=4)[:, g:g + 1].squeeze(1)

    def ht_bank_view(ap_, g):
        return ap_.rearrange("p (h q g) -> p g q h", h=2, q=4)[:, g:g + 1].squeeze(1)

    def stripe_bank_view(ap_, g):
        return ap_.rearrange("p (q g h j) -> p g q h j", q=4, g=4, h=2)[
            :, g:g + 1].squeeze(1)

    def load_pair_w(f, slot):
        for j, name in enumerate(("L0_WqT", "L0_WkT", "L0_WvT")):
            src = H[name][:, f * 128:(f + 1) * 128].rearrange("(c p) i -> p c i", p=128)
            t_, b_ = wsl[slot * 3 + j], b_wsl[slot * 3 + j]
            dst = t_.rearrange("p (c i) -> p c i", i=128)
            p.do("gpsimd", lambda g, dst=dst, src=src: g.dma_start(out=dst, in_=src),
                 writes=[b_], dma=True)

    load_pair_w(0, 0)
    p.do("sync", lambda s: s.dma_start(out=bqc, in_=H["L0_bq_c"]), writes=[b_bqc], dma=True)
    p.do("sync", lambda s: s.dma_start(out=bkc, in_=H["L0_bk_c"]), writes=[b_bkc], dma=True)
    p.do("sync", lambda s: s.dma_start(out=bvc, in_=H["L0_bv_c"]), writes=[b_bvc], dma=True)
    p.do("gpsimd", lambda g: g.dma_start(out=bo_row, in_=H["L0_bo_r"]), writes=[b_bor], dma=True)
    p.do("gpsimd", lambda g: g.memset(projbd, 0.0), writes=[b_projbd])
    for f in range(8):
        pa = H["L0_projT"]
        p.do("gpsimd", lambda g, f=f, pa=pa: g.dma_start(
            out=projbd[0:64, f * 128: f * 128 + 64], in_=pa[2 * f]), writes=[b_projbd], dma=True)
        p.do("gpsimd", lambda g, f=f, pa=pa: g.dma_start(
            out=projbd[64:128, f * 128 + 64: f * 128 + 128], in_=pa[2 * f + 1]),
            writes=[b_projbd], dma=True)
    p.do("gpsimd", lambda g: g.memset(halfbd, 0.0), writes=[b_hbd])
    p.do("gpsimd", lambda g: g.memset(halfbd[0:64, 0:1], 0.5), writes=[b_hbd])
    p.do("gpsimd", lambda g: g.memset(halfbd[64:128, 1:2], 0.5), writes=[b_hbd])
    p.do("vector", lambda v: v.tensor_scalar(out=bq_s, in0=bqc, scalar1=NRM, scalar2=None,
                                             op0=ALU.mult), reads=[b_bqc], writes=[b_bqs])
    p.do("vector", lambda v: v.tensor_scalar(out=bk_s, in0=bkc, scalar1=NRM, scalar2=None,
                                             op0=ALU.mult), reads=[b_bkc], writes=[b_bks])
    p.do("gpsimd", lambda g: g.dma_start(
        out=wo.rearrange("p (c i) -> p c i", i=D),
        in_=H["L0_WoT"].rearrange("(c p) i -> p c i", p=128)),
        writes=[b_wo], dma=True)

    layernorm(cx, ln1, b_ln1, pre_stats=True)

    def l0_proj(f, which):
        slot = f % 2
        wq_, wk_, wv_ = wsl[slot * 3], wsl[slot * 3 + 1], wsl[slot * 3 + 2]
        b_wq_, b_wk_, b_wv_ = b_wsl[slot * 3], b_wsl[slot * 3 + 1], b_wsl[slot * 3 + 2]
        qf, b_qf = qf2[f % 2], b_qf2[f % 2]
        kf, b_kf = kf2[f % 2], b_kf2[f % 2]
        vt, b_vt = vt2[f % 2], b_vt2[f % 2]
        specs = {"qk": ((wq_, b_wq_, qf, b_qf, bq_s, b_bqs, NRM),
                        (wk_, b_wk_, kf, b_kf, bk_s, b_bks, NRM)),
                 "v": ((wv_, b_wv_, q2, b_q2, bvc, b_bvc, 1.0),)}
        for (w_, b_w_, dst, b_dst, bias, b_bias, scale) in specs[which]:
            for tt in range(4):
                pb = 4 + tt % 2
                pv = cx.psum[pb].ap()[:, 0:512]
                for kc in range(ND):
                    p.do("tensor", lambda t, w_=w_, kc=kc, tt=tt, pv=pv: t.matmul(
                        pv, w_[:, kc * 128:(kc + 1) * 128], fma(ln1, kc, tt * 512, 512),
                        start=(kc == 0), stop=(kc == ND - 1)),
                        reads=[b_w_, b_ln1], writes=[cx.b_psum[pb]])
                p.do("scalar", lambda s, dst=dst, tt=tt, pv=pv, bias=bias, f=f, scale=scale:
                     s.activation(out=dst[:, tt * 512:(tt + 1) * 512], in_=pv, func=AF.Identity,
                                  bias=bias[:, f:f + 1], scale=scale),
                     reads=[cx.b_psum[pb], b_bias], writes=[b_dst])
        if which == "v":
            for tg in range(2):
                transposeN_bf(cx, [q2[:, tc * 128:(tc + 1) * 128]
                                   for tc in range(tg * 8, tg * 8 + 8)],
                              [b_q2] * 8, vt[:, tg * 1024:(tg + 1) * 1024], b_vt,
                              pb=tg % 2)

    l0_proj(0, "qk")
    l0_proj(0, "v")

    for f in range(8):
        if f < 7:
            load_pair_w(f + 1, (f + 1) % 2)
        qf, b_qf = qf2[f % 2], b_qf2[f % 2]
        kf, b_kf = kf2[f % 2], b_kf2[f % 2]
        vt, b_vt = vt2[f % 2], b_vt2[f % 2]

        for (src, b_src, dst, b_dst) in ((qf, b_qf, dq_tm, b_dqt), (kf, b_kf, dk_tm, b_dkt)):
            p.do("scalar", lambda s, src=src: s.activation(out=q2, in_=src, func=AF.Square),
                 reads=[b_src], writes=[b_q2])
            pv = cx.psum[4].ap()[:, 0:32]
            for tc in range(NT):
                p.do("tensor", lambda t, tc=tc, pv=pv: t.matmul(
                    pv[:, 2 * tc:2 * tc + 2], q2[:, tc * 128:(tc + 1) * 128], halfbd,
                    start=(tc == 0), stop=(tc == NT - 1)),
                    reads=[b_q2, b_hbd], writes=[cx.b_psum[4]])
            p.do("vector", lambda v, dst=dst, pv=pv: v.tensor_copy(out=dst, in_=pv),
                 reads=[cx.b_psum[4]], writes=[b_dst])

        for tc in range(NT):
            pb = tc % 4
            pv = cx.psum[pb].ap()[:, (tc // 4) * 128:(tc // 4) * 128 + 128]
            p.do("tensor", lambda t, tc=tc, pv=pv, f=f, kf=kf: t.matmul(
                pv, kf[:, tc * 128:(tc + 1) * 128], projbd[:, f * 128:(f + 1) * 128],
                start=(tc < 4), stop=(tc >= NT - 4)),
                reads=[b_kf, b_projbd], writes=[cx.b_psum[pb]])
        for g in range(4):
            pv4d = cx.psum[g].ap().rearrange("p (q h j) -> p q h j", h=2, j=64)
            p.do("vector", lambda v, g=g, pv4d=pv4d: v.reduce_max(
                out=ht_bank_view(mxk, g), in_=pv4d, axis=AX.X),
                reads=[cx.b_psum[g]], writes=[b_mxk])
        p.do("vector", lambda v: v.reduce_max(
            out=mxr, in_=mxk.rearrange("p (h t) -> p h t", h=2), axis=AX.X),
            reads=[b_mxk], writes=[b_mxr])
        pv5 = cx.psum[5].ap()
        p.do("tensor", lambda t: t.transpose(pv5[0:2, 256:384], mxr, cx.ident_f),
             reads=[b_mxr, cx.b_ident_f], writes=[cx.b_psum[5]])
        p.do("vector", lambda v: v.reduce_max(out=ksr2[0:2, 0:1], in_=pv5[0:2, 256:384], axis=AX.X),
             reads=[cx.b_psum[5]], writes=[b_ksr2])
        p.do("tensor", lambda t: t.transpose(pv5[0:1, 384:386], ksr2[0:2, 0:1],
                                             cx.ident_f[0:2, 0:2]),
             reads=[b_ksr2, cx.b_ident_f], writes=[cx.b_psum[5]])
        p.do("vector", lambda v: v.tensor_copy(out=ksrow[0:1, 0:2], in_=pv5[0:1, 384:386]),
             reads=[cx.b_psum[5]], writes=[b_ksrow])
        p.do("tensor", lambda t: t.matmul(pv5[:, 0:2], cx.onesf_row, ksrow[0:1, 0:2],
                                          start=True, stop=True),
             reads=[cx.b_onesf, b_ksrow], writes=[cx.b_psum[5]])
        p.do("vector", lambda v: v.tensor_copy(out=stabkb, in_=pv5[:, 0:2]),
             reads=[cx.b_psum[5]], writes=[b_stabkb])

        p.do("gpsimd", lambda v: v.tensor_scalar(
            out=biask, in0=dk_tm, scalar1=-1.0, scalar2=LNR,
            op0=ALU.mult, op1=ALU.add), reads=[b_dkt], writes=[b_biask])
        p.do("gpsimd", lambda v: v.tensor_tensor(
            out=biask.rearrange("p (t h) -> p t h", h=2),
            in0=biask.rearrange("p (t h) -> p t h", h=2),
            in1=stabkb.unsqueeze(1).broadcast_to([128, NT, 2]), op=ALU.subtract),
            reads=[b_stabkb], writes=[b_biask])
        for g in range(4):
            pv4d = cx.psum[g].ap().rearrange("p (q h j) -> p q h j", h=2, j=64)
            p.do("vector", lambda v, g=g, pv4d=pv4d: v.tensor_tensor(
                out=pv4d, in0=pv4d,
                in1=th_bank_view(biask, g).unsqueeze(3).broadcast_to([128, 4, 2, 64]),
                op=ALU.add),
                reads=[cx.b_psum[g], b_biask], writes=[cx.b_psum[g]])
            p.do("scalar", lambda s, g=g, pv4d=pv4d: s.activation(
                out=stripe_bank_view(kp, g), in_=pv4d, func=AF.Exp),
                reads=[cx.b_psum[g]], writes=[b_kp])
        p.do("vector", lambda v: v.tensor_scalar(out=kp[:, :], in0=kp[:, :], scalar1=1e-4,
                                                 scalar2=None, op0=ALU.add),
             writes=[b_kp])

        pvkv = cx.psum[4].ap()[:, 0:128]
        pvks = cx.psum[5].ap()[:, 0:1]
        for tc in range(NT):
            p.do("tensor", lambda t, tc=tc, vt=vt: t.matmul(
                pvkv, kp[:, tc * 128:(tc + 1) * 128], vt[:, tc * 128:(tc + 1) * 128],
                start=(tc == 0), stop=(tc == NT - 1)),
                reads=[b_kp, b_vt], writes=[cx.b_psum[4]])
            p.do("tensor", lambda t, tc=tc: t.matmul(
                pvks, kp[:, tc * 128:(tc + 1) * 128], cx.ones_col,
                start=(tc == 0), stop=(tc == NT - 1)),
                reads=[b_kp, cx.b_ones_col], writes=[cx.b_psum[5]])
        p.do("gpsimd", lambda g: g.memset(kvblk, 0.0), writes=[b_kvblk])
        p.do("vector", lambda v: v.tensor_copy(out=kvblk[0:64, 0:64], in_=pvkv[0:64, 0:64]),
             reads=[cx.b_psum[4]], writes=[b_kvblk])
        p.do("vector", lambda v: v.tensor_copy(out=kvblk[64:128, 64:128],
                                               in_=pvkv[64:128, 64:128]),
             reads=[cx.b_psum[4]], writes=[b_kvblk])
        p.do("vector", lambda v: v.tensor_copy(out=kvblk[0:64, 128:129], in_=pvks[0:64, :]),
             reads=[cx.b_psum[5]], writes=[b_kvblk])
        p.do("vector", lambda v: v.tensor_copy(out=kvblk[64:128, 129:130], in_=pvks[64:128, :]),
             reads=[cx.b_psum[5]], writes=[b_kvblk])
        pvt = cx.psumT[0].ap()[:, 0:128]
        p.do("tensor", lambda t: t.transpose(pvt[0:2, :], kvblk[:, 128:130], cx.ident_bf),
             reads=[b_kvblk, cx.b_ident_bf], writes=[cx.b_psumT[0]])
        p.do("vector", lambda v: v.tensor_copy(out=ksr2[0:2, :], in_=pvt[0:2, :]),
             reads=[cx.b_psumT[0]], writes=[b_ksr2])
        p.do("tensor", lambda t: t.matmul(pv5[:, 128:256], cx.onesf2, ksr2[0:2, :],
                                          start=True, stop=True),
             reads=[cx.b_onesf, b_ksr2], writes=[cx.b_psum[5]])
        p.do("vector", lambda v: v.tensor_copy(out=ksrb, in_=pv5[:, 128:256]),
             reads=[cx.b_psum[5]], writes=[b_ksrb])

        if f < 7:
            l0_proj(f + 1, "qk")

        qe, b_qe = kp, b_kp
        for tc in range(NT):
            pb = tc % 4
            pv = cx.psum[pb].ap()[:, (tc // 4) * 128:(tc // 4) * 128 + 128]
            p.do("tensor", lambda t, tc=tc, pv=pv, f=f, qf=qf: t.matmul(
                pv, qf[:, tc * 128:(tc + 1) * 128], projbd[:, f * 128:(f + 1) * 128],
                start=(tc < 4), stop=(tc >= NT - 4)),
                reads=[b_qf, b_projbd], writes=[cx.b_psum[pb]])
        for g in range(4):
            pv4d = cx.psum[g].ap().rearrange("p (q h j) -> p q h j", h=2, j=64)
            p.do("vector", lambda v, g=g, pv4d=pv4d: v.reduce_max(
                out=th_bank_view(stq_all, g), in_=pv4d, axis=AX.X),
                reads=[cx.b_psum[g]], writes=[b_stq])
        p.do("gpsimd", lambda v: v.tensor_scalar(
            out=biasq, in0=dq_tm, scalar1=-1.0, scalar2=LNR,
            op0=ALU.mult, op1=ALU.add), reads=[b_dqt], writes=[b_biasq])
        p.do("gpsimd", lambda v: v.tensor_sub(out=biasq, in0=biasq, in1=stq_all),
             reads=[b_stq], writes=[b_biasq])
        for g in range(4):
            pv4d = cx.psum[g].ap().rearrange("p (q h j) -> p q h j", h=2, j=64)
            p.do("vector", lambda v, g=g, pv4d=pv4d: v.tensor_tensor(
                out=pv4d, in0=pv4d,
                in1=th_bank_view(biasq, g).unsqueeze(3).broadcast_to([128, 4, 2, 64]),
                op=ALU.add),
                reads=[cx.b_psum[g], b_biasq], writes=[cx.b_psum[g]])
            p.do("scalar", lambda s, g=g, pv4d=pv4d: s.activation(
                out=stripe_bank_view(qe, g), in_=pv4d, func=AF.Exp),
                reads=[cx.b_psum[g]], writes=[b_qe])
        if f < 7:
            l0_proj(f + 1, "v")
        qe4 = qe.rearrange("p (t h j) -> p t h j", h=2, j=64)
        qsc4 = qsc.rearrange("p (t h j) -> p t h j", h=2, j=64)
        nrm3 = nrm_all.rearrange("p (t h) -> p t h", h=2)
        for hf in range(2):
            qeh = qe4[:, hf * 8:(hf + 1) * 8]
            p.do("vector", lambda v, qeh=qeh: v.scalar_tensor_tensor(
                out=qsc4, in0=qeh, scalar=1e-4,
                in1=ksrb.rearrange("p (h j) -> p h j", h=2).unsqueeze(1).broadcast_to(
                    [128, 8, 2, 64]), op0=ALU.add, op1=ALU.mult),
                reads=[b_qe, b_ksrb], writes=[b_qsc])
            p.do("vector", lambda v, hf=hf: v.reduce_sum(
                out=nrm3[:, hf * 8:(hf + 1) * 8], in_=qsc4, axis=AX.X),
                reads=[b_qsc], writes=[b_nrm])
            p.do("vector", lambda v, hf=hf: v.tensor_scalar(
                out=nrm_all[:, hf * 16:(hf + 1) * 16], in0=nrm_all[:, hf * 16:(hf + 1) * 16],
                scalar1=0.01, scalar2=None, op0=ALU.max), writes=[b_nrm])
            p.do("vector", lambda v, hf=hf: v.reciprocal(
                out=nrm_all[:, hf * 16:(hf + 1) * 16],
                in_=nrm_all[:, hf * 16:(hf + 1) * 16]), writes=[b_nrm])
            p.do("vector", lambda v, qeh=qeh, hf=hf: v.scalar_tensor_tensor(
                out=qsc4, in0=qeh, scalar=1e-4,
                in1=nrm3[:, hf * 8:(hf + 1) * 8].unsqueeze(3).broadcast_to(
                    [128, 8, 2, 64]), op0=ALU.add, op1=ALU.mult),
                reads=[b_qe, b_nrm], writes=[b_qsc])
            transposeN_bf(cx, [qsc[:, i * 128:(i + 1) * 128] for i in range(8)],
                          [b_qsc] * 8, qp[:, hf * 1024:(hf + 1) * 1024], b_qp, pb=hf % 2)

        for tt in range(4):
            pb = tt % 2
            pv = cx.psum[pb].ap()[:, 0:512]
            p.do("tensor", lambda t, tt=tt, pv=pv: t.matmul(
                pv, kvblk[:, 0:128], qp[:, tt * 512:(tt + 1) * 512], start=True, stop=True),
                reads=[b_kvblk, b_qp], writes=[cx.b_psum[pb]])
            p.do("scalar", lambda s, tt=tt, pv=pv, f=f: s.activation(
                out=fma(att, f, tt * 512, 512), in_=pv, func=AF.Copy),
                reads=[cx.b_psum[pb]], writes=[b_att])

    for tc in range(NT):
        for ih in range(2):
            pb = (tc * 2 + ih) % 4
            pv = cx.psum[pb].ap()[:, 0:512]
            for kc in range(ND):
                p.do("tensor", lambda t, tc=tc, ih=ih, kc=kc, pv=pv: t.matmul(
                    pv, fma(att, kc, tc * 128, 128),
                    wo[:, kc * D + ih * 512: kc * D + ih * 512 + 512],
                    start=(kc == 0), stop=False),
                    reads=[b_att, b_wo], writes=[cx.b_psum[pb]])
            p.do("tensor", lambda t, ih=ih, pv=pv: t.matmul(
                pv, cx.ones_row[:, 0:128], bo_row[:, ih * 512:(ih + 1) * 512],
                start=False, stop=True),
                reads=[cx.b_ones_row, b_bor], writes=[cx.b_psum[pb]])
            p.do("vector", lambda v, tc=tc, ih=ih, pv=pv: v.tensor_add(
                out=cx.xt[tc][:, ih * 512:(ih + 1) * 512],
                in0=cx.xt[tc][:, ih * 512:(ih + 1) * 512], in1=pv),
                reads=[cx.b_psum[pb]], writes=[cx.bx[tc]])
            if ih == 1:
                ln_stats_tile(cx, tc)





def build_l0_ffn(cx, H):
    nc, p = cx.nc, cx.p
    p.fence()
    for a in (cx.BIG1, cx.BIG2, cx.STRIPE, cx.WT, cx.SCR):
        a.reset()

    ln2 = cx.BIG1.take(BF16, 128, ND * S); b_ln2 = Buf("ln2")
    h1 = cx.STRIPE.take(BF16, 128, 4 * S); b_h1 = Buf("h1")

    wg = [cx.WT.take(BF16, 128, 8 * 512) for _ in range(2)]
    wu = [cx.WT.take(BF16, 128, 8 * 512) for _ in range(2)]
    wd = cx.WT.take(BF16, 128, 4 * 1024)
    b_wg = [Buf("wg0"), Buf("wg1")]
    b_wu = [Buf("wu0"), Buf("wu1")]
    b_wd = Buf("wd")

    A = cx.SCR
    gwf = A.take(F32, 128, 64); b_gwf = Buf("gwf")
    gwhi = A.take(BF16, 128, 64); b_gwhi = Buf("gwhi")
    gwlo = A.take(BF16, 128, 64); b_gwlo = Buf("gwlo")
    lbr = A.take(F32, 1, 8); b_lbr = Buf("lbr")
    lbbc = A.take(F32, 128, 8); b_lbbc = Buf("lbbc")
    sco128 = A.take(F32, 128, 8 * NT); b_sco = Buf("sco128")
    t128 = A.take(F32, 128, 8 * NT); b_t128 = Buf("t128")
    m1a = A.take(F32, 128, NT); b_m1 = Buf("m1a")
    m2a = A.take(F32, 128, NT); b_m2 = Buf("m2a")
    den = A.take(F32, 128, NT); b_den = Buf("den")
    swall = A.take(BF16, 128, 16 * NT); b_swall = Buf("swall")
    swv = [swall.rearrange("p (t two e) -> p two t e", two=2, e=8)[:, i:i + 1].squeeze(1)
           for i in range(2)]
    sco_all = cx.SCR2.at(2560, F32, 128, 8 * NT) if globals().get("DBG") else None
    scb_all = cx.BIG2.take(F32, 128, 8 * NT) if globals().get("DBG") else None
    b_scoall, b_scball = Buf("sco_all"), Buf("scb_all")
    c_all = A.take(F32, 128, 8 * NT); b_c = Buf("c_all")
    red = A.take(F32, 16, 16); b_red = Buf("red")
    wm = A.take(F32, 1, 8); b_wm = Buf("wm")
    wmb = A.take(F32, 128, 8); b_wmb = Buf("wmb")
    sg = A.take(BF16, 128, 1024); b_sg = Buf("sg")
    lo_t = A.take(BF16, 128, 1024); b_lot = Buf("lo_t")
    lo_f = A.take(BF16, 128, 1024); b_lof = Buf("lo_f")
    ln32 = cx.SCR2.at(4096, F32, 128, 1024); b_ln32 = Buf("ln32")

    ccin = nc.dram_tensor(f"ccin{cx.ccn}", [1, 16], F32)
    ccout = nc.dram_tensor(f"ccout{cx.ccn}", [1, 16], F32)
    cx.ccn += 1

    p.do("sync", lambda s: s.dma_start(
        out=gwf.rearrange("p (c e) -> p c e", e=8),
        in_=H["l0f_gwT"].rearrange("(c p) e -> p c e", p=128)), writes=[b_gwf], dma=True)
    p.do("sync", lambda s: s.dma_start(out=lbr, in_=H["l0f_lb"]), writes=[b_lbr], dma=True)
    p.do("vector", lambda v: v.tensor_copy(out=gwhi, in_=gwf), reads=[b_gwf], writes=[b_gwhi])
    p.do("vector", lambda v: v.tensor_sub(out=gwlo, in0=gwf, in1=gwhi),
         reads=[b_gwf, b_gwhi], writes=[b_gwlo])
    pv5 = cx.psum[5].ap()
    p.do("tensor", lambda t: t.matmul(pv5[:, 0:8], cx.onesf_row, lbr, start=True, stop=True),
         reads=[cx.b_onesf, b_lbr], writes=[cx.b_psum[5]])
    p.do("vector", lambda v: v.tensor_copy(out=lbbc, in_=pv5[:, 0:8]),
         reads=[cx.b_psum[5]], writes=[b_lbbc])

    pvG = cx.psum[4].ap()
    pvC = cx.psum[5].ap()

    stats = [None, None, None]

    def gate_cb(tc, lnd, b_lnd):
        _m, _r, _bst = stats
        p.do("vector", lambda v, tc=tc, _m=_m, _r=_r: v.tensor_scalar(
            out=ln32, in0=cx.xt[tc], scalar1=_m[:, tc:tc + 1], scalar2=_r[:, tc:tc + 1],
            op0=ALU.subtract, op1=ALU.mult), reads=[cx.bx[tc], _bst], writes=[b_ln32])
        p.do("gpsimd", lambda v: v.tensor_sub(out=lo_t, in0=ln32, in1=lnd),
             reads=[b_ln32, b_lnd], writes=[b_lot])
        transposeN_bf(cx, [lo_t[:, c * 128:(c + 1) * 128] for c in range(ND)],
                      [b_lot] * ND, lo_f, b_lof, pb=(tc + 1) % 2)
        for kc in range(ND):
            p.do("tensor", lambda t, kc=kc, tc=tc: t.matmul(
                pvG[:, tc * 8:tc * 8 + 8], fma(ln2, kc, tc * 128, 128),
                gwhi[:, kc * 8:(kc + 1) * 8],
                start=(kc == 0), stop=False),
                reads=[b_ln2, b_gwhi], writes=[cx.b_psum[4]])
        for kc in range(ND):
            p.do("tensor", lambda t, kc=kc, tc=tc: t.matmul(
                pvG[:, tc * 8:tc * 8 + 8], fma(ln2, kc, tc * 128, 128),
                gwlo[:, kc * 8:(kc + 1) * 8],
                start=False, stop=False),
                reads=[b_ln2, b_gwlo], writes=[cx.b_psum[4]])
        for kc in range(ND):
            p.do("tensor", lambda t, kc=kc, tc=tc: t.matmul(
                pvG[:, tc * 8:tc * 8 + 8], lo_f[:, kc * 128:(kc + 1) * 128],
                gwhi[:, kc * 8:(kc + 1) * 8],
                start=False, stop=(kc == ND - 1)),
                reads=[b_lof, b_gwhi], writes=[cx.b_psum[4]])

    layernorm(cx, ln2, b_ln2, per_tile_cb=gate_cb, stats_out=stats,
              pre_stats=True)

    s3 = sco128.rearrange("p (t e) -> p t e", e=8)
    t3 = t128.rearrange("p (t e) -> p t e", e=8)
    p.do("vector", lambda v: v.tensor_tensor(
        out=s3, in0=pvG[:, 0:8 * NT].rearrange("p (t e) -> p t e", e=8),
        in1=lbbc.unsqueeze(1).broadcast_to([128, NT, 8]), op=ALU.add),
        reads=[cx.b_psum[4], b_lbbc], writes=[b_sco])
    p.do("scalar", lambda s: s.activation(out=sco128, in_=sco128, func=AF.Sigmoid),
         reads=[], writes=[b_sco])
    p.do("vector", lambda v: v.reduce_max(out=m1a, in_=s3, axis=AX.X),
         reads=[b_sco], writes=[b_m1])
    p.do("vector", lambda v: v.tensor_tensor(
        out=t3, in0=s3, in1=m1a.unsqueeze(2).broadcast_to([128, NT, 8]), op=ALU.is_ge),
        reads=[b_sco, b_m1], writes=[b_t128])
    p.do("vector", lambda v: v.scalar_tensor_tensor(
        out=t128, in0=t128, scalar=-1e9, in1=sco128, op0=ALU.mult, op1=ALU.add),
        reads=[b_sco], writes=[b_t128])
    p.do("vector", lambda v: v.reduce_max(out=m2a, in_=t3, axis=AX.X),
         reads=[b_t128], writes=[b_m2])
    p.do("vector", lambda v: v.tensor_tensor(
        out=swv[0], in0=s3, in1=m2a.unsqueeze(2).broadcast_to([128, NT, 8]), op=ALU.is_ge),
        reads=[b_sco, b_m2], writes=[b_swall])
    p.do("vector", lambda v: v.tensor_add(out=den, in0=m1a, in1=m2a),
         reads=[b_m1, b_m2], writes=[b_den])
    p.do("vector", lambda v: v.tensor_scalar(out=den, in0=den, scalar1=1e-8, scalar2=None,
                                             op0=ALU.add), writes=[b_den])
    p.do("vector", lambda v: v.reciprocal(out=den, in_=den), writes=[b_den])
    p.do("vector", lambda v: v.tensor_tensor(
        out=t3, in0=s3, in1=den.unsqueeze(2).broadcast_to([128, NT, 8]), op=ALU.mult),
        reads=[b_sco, b_den], writes=[b_t128])
    p.do("vector", lambda v: v.tensor_tensor(
        out=swv[1], in0=t3, in1=swv[0], op=ALU.mult),
        reads=[b_t128], writes=[b_swall])
    for tc in range(NT):
        p.do("tensor", lambda t, tc=tc: t.matmul(
            pvC[0:16, 8:9], swall[:, tc * 16:(tc + 1) * 16], cx.ones_col,
            start=(tc == 0), stop=(tc == NT - 1)),
            reads=[b_swall, cx.b_ones_col], writes=[cx.b_psum[5]])

    p.do("vector", lambda v: v.tensor_copy(out=red[0:16, 0:1], in_=pvC[0:16, 8:9]),
         reads=[cx.b_psum[5]], writes=[b_red])
    p.do("tensor", lambda t: t.transpose(pvC[0:1, 10:26], red[0:16, 0:1],
                                         cx.ident_f[0:16, 0:16]),
         reads=[b_red, cx.b_ident_f], writes=[cx.b_psum[5]])
    p.do("vector", lambda v: v.tensor_copy(out=red[0:1, 0:16], in_=pvC[0:1, 10:26]),
         reads=[cx.b_psum[5]], writes=[b_red])
    dma1 = p.do("gpsimd", lambda g: g.dma_start(out=ccin.ap(), in_=red[0:1, 0:16]),
                reads=[b_red], dma=True)
    cc = p.do("gpsimd", lambda g: g.collective_compute(
        "AllReduce", ALU.add, replica_groups=[list(range(8))],
        ins=[ccin.ap()], outs=[ccout.ap()]), deps=[dma1])
    wmd = nc.dram_tensor(f"wmd{cx.ccn}", [1, 8], F32)

    def emit_wmean():
        p.do("gpsimd", lambda g: g.dma_start(out=red[0:1, 0:16], in_=ccout.ap()),
             deps=[cc], writes=[b_red], dma=True)
        p.do("vector", lambda v: v.tensor_scalar(out=wm, in0=red[0:1, 0:8], scalar1=1.0,
                                                 scalar2=None, op0=ALU.max),
             reads=[b_red], writes=[b_wm])
        p.do("vector", lambda v: v.reciprocal(out=wm, in_=wm), writes=[b_wm])
        p.do("vector", lambda v: v.tensor_mul(out=wm, in0=wm, in1=red[0:1, 8:16]),
             reads=[b_red], writes=[b_wm])
        d_out = p.do("sync", lambda s: s.dma_start(out=wmd.ap(), in_=wm),
                     reads=[b_wm], dma=True)
        p.do("sync", lambda s: s.dma_start(
            out=wmb, in_=wmd.ap().broadcast_to([128, 8])), writes=[b_wmb],
            deps=[d_out], dma=True)
        p.do("vector", lambda v: v.tensor_tensor(
            out=c_all.rearrange("p (t e) -> p t e", e=8),
            in0=swv[0],
            in1=wmb.unsqueeze(1).broadcast_to([128, NT, 8]), op=ALU.mult),
            reads=[b_swall, b_wmb], writes=[b_c])

    if globals().get("DBG"):
        dsel = nc.dram_tensor("d_sel", [128, 8 * NT], F32, kind="ExternalOutput").ap()
        dcal = nc.dram_tensor("d_c", [128, 8 * NT], F32, kind="ExternalOutput").ap()
        dred = nc.dram_tensor("d_red", [1, 16], F32, kind="ExternalOutput").ap()
        dwm = nc.dram_tensor("d_wm", [1, 8], F32, kind="ExternalOutput").ap()
        p.do("sync", lambda s: s.dma_start(out=dsel, in_=sel_all), reads=[b_sel], dma=True)
        p.do("sync", lambda s: s.dma_start(out=dcal, in_=c_all), reads=[b_c], dma=True)
        p.do("sync", lambda s: s.dma_start(out=dred, in_=red[0:1, 0:16]), reads=[b_red], dma=True)
        p.do("sync", lambda s: s.dma_start(out=dwm, in_=wm), reads=[b_wm], dma=True)
        dsco = nc.dram_tensor("d_sco", [128, 8 * NT], F32, kind="ExternalOutput").ap()
        p.do("sync", lambda s: s.dma_start(out=dsco, in_=sco_all), reads=[b_scoall], dma=True)
        dscb = nc.dram_tensor("d_scb", [128, 8 * NT], F32, kind="ExternalOutput").ap()
        p.do("sync", lambda s: s.dma_start(out=dscb, in_=scb_all), reads=[b_scball], dma=True)
        dln2 = nc.dram_tensor("d_ln2", [128, ND * S], BF16, kind="ExternalOutput").ap()
        p.do("sync", lambda s: s.dma_start(out=dln2, in_=ln2), reads=[b_ln2], dma=True)
        dgwhi = nc.dram_tensor("d_gwhi", [128, 64], BF16, kind="ExternalOutput").ap()
        p.do("sync", lambda s: s.dma_start(out=dgwhi, in_=gwhi), reads=[b_gwhi], dma=True)
        dgwlo = nc.dram_tensor("d_gwlo", [128, 64], BF16, kind="ExternalOutput").ap()
        p.do("sync", lambda s: s.dma_start(out=dgwlo, in_=gwlo), reads=[b_gwlo], dma=True)
        dgwf = nc.dram_tensor("d_gwf", [128, 64], F32, kind="ExternalOutput").ap()
        p.do("sync", lambda s: s.dma_start(out=dgwf, in_=gwf), reads=[b_gwf], dma=True)

    def load_expert(kind, idx, slot):
        if kind == "sh":
            srcg = H["l0f_sgwT"][:, idx * 512:(idx + 1) * 512]
            srcu = H["l0f_suwT"][:, idx * 512:(idx + 1) * 512]
        else:
            srcg = H["l0f_egwT"][idx]
            srcu = H["l0f_euwT"][idx]
        p.do("gpsimd", lambda g: g.dma_start(
            out=wg[slot].rearrange("p (c i) -> p c i", i=512),
            in_=srcg.rearrange("(c p) i -> p c i", p=128)), writes=[b_wg[slot]], dma=True)
        p.do("gpsimd", lambda g: g.dma_start(
            out=wu[slot].rearrange("p (c i) -> p c i", i=512),
            in_=srcu.rearrange("(c p) i -> p c i", p=128)), writes=[b_wu[slot]], dma=True)

    def load_expert_dw(kind, idx):
        if kind == "sh":
            srcd = H["l0f_sdwT"][idx * 512:(idx + 1) * 512, :]
        else:
            srcd = H["l0f_edwT"][idx]
        p.do("gpsimd", lambda g: g.dma_start(
            out=wd.rearrange("p (c i) -> p c i", i=1024),
            in_=srcd.rearrange("(c p) i -> p c i", p=128)), writes=[b_wd], dma=True)

    experts = [("sh", 0), ("sh", 1)] + [("rt", e) for e in range(8)]
    load_expert(*experts[0], 0)
    for ei, (kind, idx) in enumerate(experts):
        if ei == 1:
            emit_wmean()
        slot = ei % 2
        if ei + 1 < len(experts):
            load_expert(*experts[ei + 1], 1 - slot)
        load_expert_dw(kind, idx)
        for ic in range(4):
            for tt in range(4):
                pbg = (tt % 2)
                pbu = 2 + (tt % 2)
                pg = cx.psum[pbg].ap()[:, 0:512]
                pu = cx.psum[pbu].ap()[:, 0:512]
                for kc in range(ND):
                    p.do("tensor", lambda t, ic=ic, tt=tt, kc=kc, pg=pg, slot=slot: t.matmul(
                        pg, wg[slot][:, kc * 512 + ic * 128: kc * 512 + ic * 128 + 128],
                        fma(ln2, kc, tt * 512, 512),
                        start=(kc == 0), stop=(kc == ND - 1)),
                        reads=[b_wg[slot], b_ln2], writes=[cx.b_psum[pbg]])
                    p.do("tensor", lambda t, ic=ic, tt=tt, kc=kc, pu=pu, slot=slot: t.matmul(
                        pu, wu[slot][:, kc * 512 + ic * 128: kc * 512 + ic * 128 + 128],
                        fma(ln2, kc, tt * 512, 512),
                        start=(kc == 0), stop=(kc == ND - 1)),
                        reads=[b_wu[slot], b_ln2], writes=[cx.b_psum[pbu]])
                sgh = sg[:, (tt % 2) * 512:(tt % 2) * 512 + 512]
                p.do("scalar", lambda s, sgh=sgh, pg=pg: s.activation(
                    out=sgh, in_=pg, func=AF.Silu),
                    reads=[cx.b_psum[pbg]], writes=[b_sg])
                p.do("vector", lambda v, ic=ic, tt=tt, sgh=sgh, pu=pu: v.tensor_tensor(
                    out=h1[:, ic * S + tt * 512: ic * S + tt * 512 + 512],
                    in0=pu, in1=sgh, op=ALU.mult),
                    reads=[cx.b_psum[pbu], b_sg], writes=[b_h1])
        for tc in range(NT):
            for dc in range(2):
                pbd = 4 + (tc * 2 + dc) % 2
                pd = cx.psum[pbd].ap()[:, 0:512]
                for ic in range(4):
                    p.do("tensor", lambda t, tc=tc, dc=dc, ic=ic, pd=pd: t.matmul(
                        pd, h1[:, ic * S + tc * 128: ic * S + tc * 128 + 128],
                        wd[:, ic * 1024 + dc * 512: ic * 1024 + dc * 512 + 512],
                        start=(ic == 0), stop=(ic == 3)),
                        reads=[b_h1, b_wd], writes=[cx.b_psum[pbd]])
                if kind == "sh":
                    p.do("vector", lambda v, tc=tc, dc=dc, pd=pd: v.tensor_add(
                        out=cx.xt[tc][:, dc * 512:(dc + 1) * 512],
                        in0=cx.xt[tc][:, dc * 512:(dc + 1) * 512], in1=pd),
                        reads=[cx.b_psum[pbd]], writes=[cx.bx[tc]])
                else:
                    p.do("vector", lambda v, tc=tc, dc=dc, pd=pd, idx=idx: v.scalar_tensor_tensor(
                        out=cx.xt[tc][:, dc * 512:(dc + 1) * 512],
                        in0=pd, scalar=c_all[:, tc * 8 + idx: tc * 8 + idx + 1],
                        in1=cx.xt[tc][:, dc * 512:(dc + 1) * 512],
                        op0=ALU.mult, op1=ALU.add),
                        reads=[cx.b_psum[pbd], b_c], writes=[cx.bx[tc]])
                if dc == 1 and ei == len(experts) - 1:
                    ln_stats_tile(cx, tc)





def build_l1_attn(cx, H):
    nc, p = cx.nc, cx.p
    p.fence()
    for a in (cx.BIG1, cx.BIG2, cx.STRIPE, cx.WT, cx.SCR):
        a.reset()

    ln3 = cx.BIG1.take(BF16, 128, ND * S); b_ln3 = Buf("ln3")
    rem = cx.BIG2.take(BF16, 128, ND * S); b_rem = Buf("rem")

    A = cx.SCR
    apool = A.take(BF16, 128, 4); b_apool = Buf("apool")
    pooled = A.take(BF16, 128, 512); b_pooled = Buf("pooled")
    rkvf = A.take(BF16, 128, 256); b_rkvf = Buf("rkvf")
    rvT = A.take(BF16, 128, 128); b_rvT = Buf("rvT")
    qkvb = A.take(F32, 128, 24); b_qkvb = Buf("qkvb")
    qb_s = A.take(F32, 128, 8); b_qbs = Buf("qb_s")
    rqb = A.take(F32, 128, 2); b_rqb = Buf("rqb")
    rqb_s = A.take(F32, 128, 2); b_rqbs = Buf("rqb_s")
    rkvb = A.take(F32, 128, 4); b_rkvb = Buf("rkvb")
    routb = A.take(F32, 128, 8); b_routb = Buf("routb")
    outb_row = A.take(BF16, 1, D); b_outbr = Buf("outb_row")
    glt = A.take(BF16, 128, 128); b_glt = Buf("glt")
    grt = A.take(BF16, 128, 128); b_grt = Buf("grt")
    gg = A.take(F32, 128, 32); b_gg = Buf("gg")
    g1 = A.take(F32, 128, 32); b_g1 = Buf("g1")
    rs22 = [A.take(F32, 128, 2) for _ in range(2)]
    b_rs22 = [Buf("rs20"), Buf("rs21")]
    letile2 = [A.take(BF16, 128, 128) for _ in range(2)]
    b_let2 = [Buf("let0"), Buf("let1")]
    locmask = A.take(BF16, 128, 256); b_locm = Buf("locmask")
    latmask = A.take(BF16, 128, NT * 64); b_latm = Buf("latmask")

    p.do("sync", lambda s: s.dma_start(out=qkvb, in_=H["l1_qkvb_c"]), writes=[b_qkvb], dma=True)
    p.do("sync", lambda s: s.dma_start(out=rqb, in_=H["l1_rqb_c"]), writes=[b_rqb], dma=True)
    p.do("sync", lambda s: s.dma_start(out=rkvb, in_=H["l1_rkvb_c"]), writes=[b_rkvb], dma=True)
    p.do("sync", lambda s: s.dma_start(out=routb, in_=H["l1_routb_c"]), writes=[b_routb], dma=True)
    p.do("gpsimd", lambda g: g.dma_start(out=outb_row, in_=H["l1_outb_r"]), writes=[b_outbr], dma=True)
    p.do("vector", lambda v: v.tensor_scalar(out=qb_s, in0=qkvb[:, 0:8], scalar1=0.125,
                                             scalar2=None, op0=ALU.mult),
         reads=[b_qkvb], writes=[b_qbs])
    p.do("vector", lambda v: v.tensor_scalar(out=rqb_s, in0=rqb, scalar1=0.125,
                                             scalar2=None, op0=ALU.mult),
         reads=[b_rqb], writes=[b_rqbs])
    glr = A.take(BF16, 1, 128); b_glr = Buf("glr")
    grr = A.take(BF16, 1, 128); b_grr = Buf("grr")
    p.do("gpsimd", lambda g: g.dma_start(out=glr, in_=H["l1_gl_r"]), writes=[b_glr], dma=True)
    p.do("gpsimd", lambda g: g.dma_start(out=grr, in_=H["l1_gr_r"]), writes=[b_grr], dma=True)
    pv5 = cx.psum[5].ap()
    p.do("tensor", lambda t: t.matmul(pv5[:, 0:128], cx.ones_row[:, 0:128], glr,
                                      start=True, stop=False),
         reads=[cx.b_ones_row, b_glr], writes=[cx.b_psum[5]])
    p.do("tensor", lambda t: t.matmul(pv5[:, 128:256], cx.ones_row[:, 0:128], grr,
                                      start=False, stop=True),
         reads=[cx.b_ones_row, b_grr], writes=[cx.b_psum[5]])
    p.do("vector", lambda v: v.tensor_copy(out=glt, in_=pv5.bitcast(F32)[:, 0:128] if False else pv5[:, 0:128]),
         reads=[cx.b_psum[5]], writes=[b_glt])
    p.do("vector", lambda v: v.tensor_copy(out=grt, in_=pv5[:, 128:256]),
         reads=[cx.b_psum[5]], writes=[b_grt])
    p.do("gpsimd", lambda g: g.memset(locmask, 1.0), writes=[b_locm])
    p.do("gpsimd", lambda g: g.affine_select(locmask[:, 0:128], locmask[:, 0:128],
                                             [[-1, 128]], ALU.is_ge, 0.0,
                                             base=0, channel_multiplier=1), writes=[b_locm])
    p.do("gpsimd", lambda g: g.affine_select(locmask[:, 128:256], locmask[:, 128:256],
                                             [[1, 128]], ALU.is_ge, 0.0,
                                             base=-1, channel_multiplier=-1), writes=[b_locm])
    p.do("gpsimd", lambda g: g.memset(latmask, 1.0), writes=[b_latm])
    for tc in range(NT):
        p.do("gpsimd", lambda g, tc=tc: g.affine_select(
            latmask[:, tc * 64:(tc + 1) * 64], latmask[:, tc * 64:(tc + 1) * 64],
            [[-32, 64]], ALU.is_ge, 0.0, base=128 * tc - 32, channel_multiplier=1),
            writes=[b_latm])
    p.do("gpsimd", lambda g: g.memset(apool, 1.0 / 32.0), writes=[b_apool])
    p.do("gpsimd", lambda g: g.affine_select(apool, apool, [[-32, 4]], ALU.is_ge, 0.0,
                                             base=0, channel_multiplier=1), writes=[b_apool])
    p.do("gpsimd", lambda g: g.affine_select(apool, apool, [[32, 4]], ALU.is_ge, 0.0,
                                             base=31, channel_multiplier=-1), writes=[b_apool])

    wrq = cx.WT.take(BF16, 128, ND * 256); b_wrq = Buf("wrq")
    wrkv = cx.WT.take(BF16, 128, ND * 512); b_wrkv = Buf("wrkv")
    wrout = cx.WT.take(BF16, 128, 2 * D); b_wrout = Buf("wrout")
    p.do("gpsimd", lambda g: g.dma_start(
        out=wrq.rearrange("p (c i) -> p c i", i=256),
        in_=H["l1_rqw"].rearrange("(c p) i -> p c i", p=128)), writes=[b_wrq], dma=True)
    p.do("gpsimd", lambda g: g.dma_start(
        out=wrkv.rearrange("p (c i) -> p c i", i=512),
        in_=H["l1_rkvw"].rearrange("(c p) i -> p c i", p=128)), writes=[b_wrkv], dma=True)
    p.do("gpsimd", lambda g: g.dma_start(
        out=wrout.rearrange("p (c i) -> p c i", i=D),
        in_=H["l1_routw"].rearrange("(c p) i -> p c i", p=128)), writes=[b_wrout], dma=True)

    pv4 = cx.psum[4].ap()

    def pool_cb(tc, lnd, b_lnd):
        for dc in range(ND):
            p.do("tensor", lambda t, tc=tc, dc=dc: t.matmul(
                pv4[:, dc * 64 + 4 * tc: dc * 64 + 4 * tc + 4],
                lnd[:, dc * 128:(dc + 1) * 128], apool,
                start=(tc == 0 and dc == 0), stop=(tc == NT - 1 and dc == ND - 1)),
                reads=[b_lnd, b_apool], writes=[cx.b_psum[4]])

    layernorm(cx, ln3, b_ln3, per_tile_cb=pool_cb, pre_stats=True)
    p.do("vector", lambda v: v.tensor_copy(out=pooled, in_=pv4[:, 0:512]),
         reads=[cx.b_psum[4]], writes=[b_pooled])
    if globals().get("L1STAGE", 99) < 2:
        return

    rqf = cx.STRIPE.take(BF16, 128, 2 * S); b_rqf = Buf("rqf")
    latn = cx.STRIPE.take(BF16, 128, 2 * S); b_latn = Buf("latn")
    for oc in range(2):
        for tt in range(4):
            pb = tt % 2
            pv = cx.psum[pb].ap()[:, 0:512]
            for kc in range(ND):
                p.do("tensor", lambda t, oc=oc, tt=tt, kc=kc, pv=pv: t.matmul(
                    pv, wrq[:, kc * 256 + oc * 128: kc * 256 + oc * 128 + 128],
                    fma(ln3, kc, tt * 512, 512),
                    start=(kc == 0), stop=(kc == ND - 1)),
                    reads=[b_wrq, b_ln3], writes=[cx.b_psum[pb]])
            p.do("scalar", lambda s, oc=oc, tt=tt, pv=pv: s.activation(
                out=rqf[:, oc * S + tt * 512: oc * S + tt * 512 + 512], in_=pv,
                func=AF.Identity, bias=rqb_s[:, oc:oc + 1], scale=0.125),
                reads=[cx.b_psum[pb], b_rqbs], writes=[b_rqf])

    if globals().get("L1STAGE", 99) < 3:
        return
    for oc in range(4):
        pv = cx.psum[5].ap()[:, oc * 64:(oc + 1) * 64]
        for kc in range(ND):
            p.do("tensor", lambda t, oc=oc, kc=kc, pv=pv: t.matmul(
                pv, wrkv[:, kc * 512 + oc * 128: kc * 512 + oc * 128 + 128],
                pooled[:, kc * 64:(kc + 1) * 64],
                start=(kc == 0), stop=(kc == ND - 1)),
                reads=[b_wrkv, b_pooled], writes=[cx.b_psum[5]])
        p.do("scalar", lambda s, oc=oc, pv=pv: s.activation(
            out=rkvf[:, oc * 64:(oc + 1) * 64], in_=pv, func=AF.Identity,
            bias=rkvb[:, oc:oc + 1]),
            reads=[cx.b_psum[5], b_rkvb], writes=[b_rkvf])

    letP2 = [cx.SCR2.at(4096, BF16, 128, 512), cx.SCR2.at(5120, BF16, 128, 512)]
    b_letP2 = [Buf("letP0"), Buf("letP1")]
    latsum2 = [A.take(F32, 128, 8) for _ in range(2)]
    b_latsum2 = [Buf("lats0"), Buf("lats1")]
    for tp in range(NT // 2):
        letP, b_letP = letP2[tp % 2], b_letP2[tp % 2]
        lsum, b_lsum = latsum2[tp % 2], b_latsum2[tp % 2]
        boff = 256 * (tp % 2)
        for t2 in range(2):
            tc = 2 * tp + t2
            for lh in range(4):
                hh, ch = lh % 2, lh // 2
                pv = cx.psum[hh].ap()
                p.do("tensor", lambda t, tc=tc, t2=t2, pv=pv, hh=hh, ch=ch,
                     boff=boff: t.matmul(
                        pv[:, boff + t2 * 128 + ch * 64: boff + t2 * 128 + ch * 64 + 64],
                        rqf[64 * hh: 64 * hh + 64, ch * S + tc * 128: ch * S + tc * 128 + 128],
                        rkvf[64 * hh: 64 * hh + 64, ch * 64: ch * 64 + 64],
                        start=True, stop=True),
                    reads=[b_rqf, b_rkvf], writes=[cx.b_psum[hh]])
        for hh in range(2):
            pv = cx.psum[hh].ap()
            p.do("scalar", lambda s, pv=pv, hh=hh, letP=letP, boff=boff: s.activation(
                out=letP.rearrange("p (t2 ch hh j) -> p hh t2 ch j", t2=2, ch=2,
                                   j=64)[:, hh:hh + 1].squeeze(1),
                in_=pv[:, boff:boff + 256].rearrange("p (t2 ch j) -> p t2 ch j",
                                                     t2=2, ch=2),
                func=AF.Exp),
                reads=[cx.b_psum[hh]], writes=[b_letP])
        le4 = letP.rearrange("p (t2 lh j) -> p t2 lh j", t2=2, lh=4)
        p.do("vector", lambda v, tp=tp, le4=le4: v.tensor_tensor(
            out=le4, in0=le4,
            in1=latmask.rearrange("p (t j) -> p t j", j=64)[:, 2 * tp:2 * tp + 2]
            .unsqueeze(2).broadcast_to([128, 2, 4, 64]), op=ALU.mult),
            reads=[b_latm], writes=[b_letP])
        p.do("vector", lambda v, le4=le4, lsum=lsum: v.reduce_sum(
            out=lsum.rearrange("p (t2 lh) -> p t2 lh", t2=2), in_=le4, axis=AX.X),
            reads=[b_letP], writes=[b_lsum])
        p.do("vector", lambda v, lsum=lsum: v.tensor_scalar(
            out=lsum, in0=lsum, scalar1=1e-30, scalar2=None, op0=ALU.add),
            writes=[b_lsum])
        p.do("vector", lambda v, lsum=lsum: v.reciprocal(out=lsum, in_=lsum),
             writes=[b_lsum])
        p.do("vector", lambda v, le4=le4, lsum=lsum: v.tensor_tensor(
            out=le4, in0=le4,
            in1=lsum.rearrange("p (t2 lh) -> p t2 lh", t2=2).unsqueeze(3)
            .broadcast_to([128, 2, 4, 64]), op=ALU.mult),
            reads=[b_lsum], writes=[b_letP])
        for ch in range(2):
            transposeN_bf(cx, [letP[:, t2 * 256 + ch * 128: t2 * 256 + ch * 128 + 128]
                               for t2 in range(2)], [b_letP] * 2,
                          latn[:, ch * S + 2 * tp * 128: ch * S + 2 * tp * 128 + 256],
                          b_latn, pb=tp % 2)

    if globals().get("L1STAGE", 99) < 4:
        return
    for i in range(2):
        pvt = cx.psumT[i % 2].ap()[0:64, 0:128]
        p.do("tensor", lambda t, i=i, pvt=pvt: t.transpose(
            pvt, rkvf[:, (2 + i) * 64: (3 + i) * 64], cx.ident_bf),
            reads=[b_rkvf, cx.b_ident_bf], writes=[cx.b_psumT[i % 2]])
        p.do("vector", lambda v, i=i, pvt=pvt: v.tensor_copy(
            out=rvT[0:64, i * 64: i * 64 + 64], in_=pvt[0:64, 0:64]),
            reads=[cx.b_psumT[i % 2]], writes=[b_rvT])
        p.do("vector", lambda v, i=i, pvt=pvt: v.tensor_copy(
            out=rvT[64:128, i * 64: i * 64 + 64], in_=pvt[0:64, 64:128]),
            reads=[cx.b_psumT[i % 2]], writes=[b_rvT])

    latf = rqf
    for lh in range(4):
        hh, ch = lh % 2, lh // 2
        for tt in range(4):
            pb = 2 + tt % 2
            pv = cx.psum[pb].ap()[0:64, 0:512]
            p.do("tensor", lambda t, lh=lh, tt=tt, pv=pv, hh=hh, ch=ch: t.matmul(
                pv, rvT[64 * hh: 64 * hh + 64, ch * 64: ch * 64 + 64],
                latn[64 * hh: 64 * hh + 64, ch * S + tt * 512: ch * S + tt * 512 + 512],
                start=True, stop=True),
                reads=[b_rvT, b_latn], writes=[cx.b_psum[pb]])
            p.do("scalar", lambda s, lh=lh, tt=tt, pv=pv, hh=hh, ch=ch: s.activation(
                out=latf[64 * hh: 64 * hh + 64, ch * S + tt * 512: ch * S + tt * 512 + 512],
                in_=pv, func=AF.Copy),
                reads=[cx.b_psum[pb]], writes=[b_rqf])

    for oc in range(ND):
        for tt in range(4):
            pb = tt % 2
            pv = cx.psum[pb].ap()[:, 0:512]
            for kc in range(2):
                p.do("tensor", lambda t, oc=oc, tt=tt, kc=kc, pv=pv: t.matmul(
                    pv, wrout[:, kc * D + oc * 128: kc * D + oc * 128 + 128],
                    latf[:, kc * S + tt * 512: kc * S + tt * 512 + 512],
                    start=(kc == 0), stop=(kc == 1)),
                    reads=[b_wrout, b_rqf], writes=[cx.b_psum[pb]])
            p.do("scalar", lambda s, oc=oc, tt=tt, pv=pv: s.activation(
                out=fma(rem, oc, tt * 512, 512), in_=pv, func=AF.Identity,
                bias=routb[:, oc:oc + 1]),
                reads=[cx.b_psum[pb], b_routb], writes=[b_rem])

    if globals().get("DBG"):
        for nm, ap_, b_ in (("d_pooled", pooled, b_pooled), ("d_rkvf", rkvf, b_rkvf),
                            ("d_latn", latn, b_latn), ("d_latf", latf, b_rqf),
                            ("d_rvT", rvT, b_rvT)):
            dt_ = nc.dram_tensor(nm, [ap_.shape[0], ap_.shape[1]], BF16,
                                 kind="ExternalOutput").ap()
            p.do("sync", lambda s, dt_=dt_, ap_=ap_: s.dma_start(out=dt_, in_=ap_),
                 reads=[b_], dma=True)
        drem = nc.dram_tensor("d_rem", [128, ND * S], BF16, kind="ExternalOutput").ap()
        p.do("sync", lambda s: s.dma_start(out=drem, in_=rem), reads=[b_rem], dma=True)

    if globals().get("L1STAGE", 99) < 5:
        return
    p.fence()
    cx.STRIPE.reset()
    cx.WT.reset()
    wout = cx.WT.take(BF16, 128, ND * D); b_wout = Buf("wout")
    p.do("gpsimd", lambda g: g.dma_start(
        out=wout.rearrange("p (c i) -> p c i", i=D),
        in_=H["l1_outw"].rearrange("(c p) i -> p c i", p=128)), writes=[b_wout], dma=True)
    qf2 = [cx.STRIPE.take(BF16, 128, S), cx.WT.take(BF16, 128, S)]
    b_qf2 = [Buf("l1qf0"), Buf("l1qf1")]
    kf2 = [cx.STRIPE.take(BF16, 128, S), cx.WT.take(BF16, 128, S)]
    b_kf2 = [Buf("l1kf0"), Buf("l1kf1")]
    vt2 = [cx.STRIPE.take(BF16, 128, S), cx.WT.take(BF16, 128, S)]
    b_vt2 = [Buf("l1vt0"), Buf("l1vt1")]
    locf = cx.STRIPE.take(BF16, 128, S); b_loc = Buf("locf")
    remt = cx.STRIPE.take(BF16, 128, S); b_remt = Buf("remt")
    expst4 = [cx.SCR2.at(4096 + i * 1024, BF16, 128, 512) for i in range(4)]
    b_expst4 = [Buf(f"expst{i}") for i in range(4)]
    rs8g2 = [A.take(F32, 128, 8) for _ in range(2)]
    b_rs8g2 = [Buf("rs8g0"), Buf("rs8g1")]

    wsl = [cx.WT.take(BF16, 128, ND * 128) for _ in range(6)]
    b_wsl = [Buf(f"l1w{i}") for i in range(6)]

    def load_pair_w(f, slot):
        for j, coloff in enumerate((0, D, 2 * D)):
            src = H["l1_qkvw"][:, coloff + f * 128: coloff + (f + 1) * 128]
            src = src.rearrange("(c p) i -> p c i", p=128)
            t_, b_ = wsl[slot * 3 + j], b_wsl[slot * 3 + j]
            p.do("gpsimd", lambda g, t_=t_, src=src: g.dma_start(
                out=t_.rearrange("p (c i) -> p c i", i=128), in_=src),
                writes=[b_], dma=True)

    def l1_proj(f, which):
        slot = f % 2
        wq_, wk_, wv_ = wsl[slot * 3], wsl[slot * 3 + 1], wsl[slot * 3 + 2]
        b_wq_, b_wk_, b_wv_ = b_wsl[slot * 3], b_wsl[slot * 3 + 1], b_wsl[slot * 3 + 2]
        qf, b_qf = qf2[f % 2], b_qf2[f % 2]
        kf, b_kf = kf2[f % 2], b_kf2[f % 2]
        vt, b_vt = vt2[f % 2], b_vt2[f % 2]
        specs = {"qk": ((wq_, b_wq_, qf, b_qf, qb_s, b_qbs, 0.125, f),
                        (wk_, b_wk_, kf, b_kf, qkvb, b_qkvb, 1.0, 8 + f)),
                 "v": ((wv_, b_wv_, locf, b_loc, qkvb, b_qkvb, 1.0, 16 + f),)}
        for (w_, b_w_, dst, b_dst, bias, b_bias, scale, bcol) in specs[which]:
            for tt in range(4):
                pb = tt % 2
                pv = cx.psum[pb].ap()[:, 0:512]
                for kc in range(ND):
                    p.do("tensor", lambda t, w_=w_, kc=kc, tt=tt, pv=pv: t.matmul(
                        pv, w_[:, kc * 128:(kc + 1) * 128], fma(ln3, kc, tt * 512, 512),
                        start=(kc == 0), stop=(kc == ND - 1)),
                        reads=[b_w_, b_ln3], writes=[cx.b_psum[pb]])
                p.do("scalar", lambda s, dst=dst, tt=tt, pv=pv, bias=bias, bcol=bcol,
                     scale=scale: s.activation(
                        out=dst[:, tt * 512:(tt + 1) * 512], in_=pv, func=AF.Identity,
                        bias=bias[:, bcol:bcol + 1], scale=scale),
                     reads=[cx.b_psum[pb], b_bias], writes=[b_dst])
        if which == "v":
            for tg in range(2):
                transposeN_bf(cx, [locf[:, tc * 128:(tc + 1) * 128]
                                   for tc in range(tg * 8, tg * 8 + 8)],
                              [b_loc] * 8, vt[:, tg * 1024:(tg + 1) * 1024], b_vt,
                              pb=tg % 2)

    load_pair_w(0, 0)
    l1_proj(0, "qk")
    l1_proj(0, "v")

    for f in range(8):
        if f < 7:
            load_pair_w(f + 1, (f + 1) % 2)
        qf, b_qf = qf2[f % 2], b_qf2[f % 2]
        kf, b_kf = kf2[f % 2], b_kf2[f % 2]
        vt, b_vt = vt2[f % 2], b_vt2[f % 2]
        gtmp = vt

        if globals().get("L1STAGE", 99) < 6:
            continue
        for g in range(4):
            rs8, b_rs8 = rs8g2[g % 2], b_rs8g2[g % 2]
            for i in range(4):
                qc = 4 * g + i
                expst, b_expst = expst4[i], b_expst4[i]
                for hh in range(2):
                    pbS = 2 * hh + i // 2
                    pvS = cx.psum[pbS].ap()
                    off = (i % 2) * 256
                    for cc in range(2):
                        if qc == 0 and cc == 0:
                            continue
                        kt = qc - 1 + cc
                        p.do("tensor", lambda t, hh=hh, cc=cc, kt=kt, qc=qc, pvS=pvS,
                             off=off, kf=kf, qf=qf: t.matmul(
                                pvS[:, off + cc * 128: off + cc * 128 + 128],
                                kf[64 * hh: 64 * hh + 64, kt * 128: kt * 128 + 128],
                                qf[64 * hh: 64 * hh + 64, qc * 128: qc * 128 + 128],
                                start=(cc == 0 or qc == 0), stop=(cc == 1)),
                            reads=[b_kf, b_qf], writes=[cx.b_psum[pbS]])
                    p.do("scalar", lambda s, pvS=pvS, off=off, hh=hh, expst=expst:
                         s.activation(out=expst[:, hh * 256: hh * 256 + 256],
                                      in_=pvS[:, off: off + 256], func=AF.Exp),
                         reads=[cx.b_psum[pbS]], writes=[b_expst])
                if qc == 0:
                    for hh_ in range(2):
                        p.do("vector", lambda v, expst=expst, hh_=hh_: v.memset(
                            expst[:, hh_ * 256: hh_ * 256 + 128], 0.0), writes=[b_expst])
                p.do("vector", lambda v, expst=expst: v.tensor_tensor(
                    out=expst.rearrange("p (h i) -> p h i", h=2),
                    in0=expst.rearrange("p (h i) -> p h i", h=2),
                    in1=locmask.unsqueeze(1).broadcast_to([128, 2, 256]), op=ALU.mult),
                    reads=[b_locm], writes=[b_expst])
            for i in range(4):
                qc = 4 * g + i
                pbP = 4 + i // 2
                off = (i % 2) * 256
                pvP = cx.psum[pbP].ap()
                expst, b_expst = expst4[i], b_expst4[i]
                first = True
                for hh in range(2):
                    for cc in range(2):
                        if qc == 0 and cc == 0:
                            continue
                        kt = qc - 1 + cc
                        p.do("tensor", lambda t, hh=hh, cc=cc, kt=kt, pvP=pvP, off=off,
                             first=first, expst=expst, vt=vt: t.matmul(
                                pvP[:, off + hh * 128: off + hh * 128 + 64],
                                expst[:, (hh * 2 + cc) * 128:(hh * 2 + cc) * 128 + 128],
                                vt[:, kt * 128 + hh * 64: kt * 128 + hh * 64 + 64],
                                start=first, stop=False),
                            reads=[b_expst, b_vt], writes=[cx.b_psum[pbP]])
                        p.do("tensor", lambda t, hh=hh, cc=cc, pvP=pvP, off=off,
                             expst=expst: t.matmul(
                                pvP[:, off + hh * 128 + 64: off + hh * 128 + 65],
                                expst[:, (hh * 2 + cc) * 128:(hh * 2 + cc) * 128 + 128],
                                cx.ones_col,
                                start=False, stop=(hh == 1 and cc == 1)),
                            reads=[b_expst, cx.b_ones_col], writes=[cx.b_psum[pbP]])
                        first = False
            for b in range(2):
                pvP = cx.psum[4 + b].ap()
                p.do("vector", lambda v, b=b, pvP=pvP, rs8=rs8: v.tensor_copy(
                    out=rs8[:, b * 4:(b + 1) * 4],
                    in_=pvP.rearrange("p (x c) -> p x c", c=128)[:, :, 64]),
                    reads=[cx.b_psum[4 + b]], writes=[b_rs8])
            p.do("vector", lambda v, rs8=rs8: v.tensor_scalar(
                out=rs8, in0=rs8, scalar1=1e-30, scalar2=None, op0=ALU.add),
                writes=[b_rs8])
            p.do("vector", lambda v, rs8=rs8: v.reciprocal(out=rs8, in_=rs8),
                 writes=[b_rs8])
            for b in range(2):
                pvP = cx.psum[4 + b].ap()
                qcA = 4 * g + 2 * b
                p.do("vector", lambda v, b=b, pvP=pvP, rs8=rs8, qcA=qcA: v.tensor_tensor(
                    out=locf[:, qcA * 128: qcA * 128 + 256].rearrange(
                        "p (x d) -> p x d", d=64),
                    in0=pvP.rearrange("p (x c) -> p x c", c=128)[:, :, 0:64],
                    in1=rs8[:, b * 4:(b + 1) * 4].unsqueeze(2).broadcast_to([128, 4, 64]),
                    op=ALU.mult),
                    reads=[cx.b_psum[4 + b], b_rs8], writes=[b_loc])

        if globals().get("L1STAGE", 99) < 7:
            continue
        if f < 7:
            l1_proj(f + 1, "qk")
        for tg in range(2):
            transposeN_bf(cx, [fma(rem, f, tc * 128, 128)
                               for tc in range(tg * 8, tg * 8 + 8)],
                          [b_rem] * 8, remt[:, tg * 1024:(tg + 1) * 1024], b_remt, pb=tg % 2)
        p.do("gpsimd", lambda v, gtmp=gtmp: v.tensor_tensor(
            out=gtmp.rearrange("p (q i) -> p q i", q=NT),
            in0=locf.rearrange("p (q i) -> p q i", q=NT),
            in1=glt.unsqueeze(1).broadcast_to([128, NT, 128]), op=ALU.mult),
            reads=[b_loc, b_glt], writes=[b_vt])
        p.do("vector", lambda v, gtmp=gtmp: v.reduce_sum(
            out=g1.rearrange("p (q h) -> p q h", q=NT),
            in_=gtmp.rearrange("p (q h d) -> p q h d", q=NT, h=2), axis=AX.X),
            reads=[b_vt], writes=[b_g1])
        p.do("gpsimd", lambda v, gtmp=gtmp: v.tensor_tensor(
            out=gtmp.rearrange("p (q i) -> p q i", q=NT),
            in0=remt.rearrange("p (q i) -> p q i", q=NT),
            in1=grt.unsqueeze(1).broadcast_to([128, NT, 128]), op=ALU.mult),
            reads=[b_remt, b_grt], writes=[b_vt])
        p.do("vector", lambda v, gtmp=gtmp: v.reduce_sum(
            out=gg.rearrange("p (q h) -> p q h", q=NT),
            in_=gtmp.rearrange("p (q h d) -> p q h d", q=NT, h=2), axis=AX.X),
            reads=[b_vt], writes=[b_gg])
        p.do("vector", lambda v: v.tensor_add(out=gg, in0=gg, in1=g1),
             reads=[b_g1], writes=[b_gg])
        p.do("scalar", lambda s: s.activation(out=gg, in_=gg, func=AF.Sigmoid),
             reads=[], writes=[b_gg])
        p.do("vector", lambda v: v.tensor_sub(out=locf[:, :], in0=locf[:, :], in1=remt[:, :]),
             reads=[b_remt], writes=[b_loc])
        p.do("gpsimd", lambda v: v.tensor_tensor(
            out=locf.rearrange("p (q h d) -> p q h d", q=NT, h=2),
            in0=locf.rearrange("p (q h d) -> p q h d", q=NT, h=2),
            in1=gg.rearrange("p (q h) -> p q h", q=NT).unsqueeze(3).broadcast_to(
                [128, NT, 2, 64]), op=ALU.mult),
            reads=[b_gg], writes=[b_loc])
        p.do("vector", lambda v: v.tensor_add(out=remt[:, :], in0=remt[:, :], in1=locf[:, :]),
             reads=[b_loc], writes=[b_remt])
        for tg in range(2):
            transposeN_bf(cx, [remt[:, tc * 128:(tc + 1) * 128]
                               for tc in range(tg * 8, tg * 8 + 8)],
                          [b_remt] * 8, fma(rem, f, tg * 1024, 1024), b_rem, pb=tg % 2)
        if f < 7:
            l1_proj(f + 1, "v")

    if globals().get("L1STAGE", 99) < 8:
        return
    if globals().get("DBG"):
        dmix = nc.dram_tensor("d_mix", [128, ND * S], BF16, kind="ExternalOutput").ap()
        p.do("sync", lambda s: s.dma_start(out=dmix, in_=rem), reads=[b_rem], dma=True)
        for nm, ap_, b_ in (("d_qf7", qf, b_qf), ("d_kf7", kf, b_kf),
                            ("d_vt7", vt, b_vt), ("d_loc7", locf, b_loc)):
            dt_ = nc.dram_tensor(nm, [128, S], BF16, kind="ExternalOutput").ap()
            p.do("sync", lambda s, dt_=dt_, ap_=ap_: s.dma_start(out=dt_, in_=ap_),
                 reads=[b_], dma=True)

    for tc in range(NT):
        for ih in range(2):
            pb = (tc * 2 + ih) % 4
            pv = cx.psum[pb].ap()[:, 0:512]
            for kc in range(ND):
                p.do("tensor", lambda t, tc=tc, ih=ih, kc=kc, pv=pv: t.matmul(
                    pv, fma(rem, kc, tc * 128, 128),
                    wout[:, kc * D + ih * 512: kc * D + ih * 512 + 512],
                    start=(kc == 0), stop=False),
                    reads=[b_rem, b_wout], writes=[cx.b_psum[pb]])
            p.do("tensor", lambda t, ih=ih, pv=pv: t.matmul(
                pv, cx.ones_row[:, 0:128], outb_row[:, ih * 512:(ih + 1) * 512],
                start=False, stop=True),
                reads=[cx.b_ones_row, b_outbr], writes=[cx.b_psum[pb]])
            p.do("vector", lambda v, tc=tc, ih=ih, pv=pv: v.tensor_add(
                out=cx.xt[tc][:, ih * 512:(ih + 1) * 512],
                in0=cx.xt[tc][:, ih * 512:(ih + 1) * 512], in1=pv),
                reads=[cx.b_psum[pb]], writes=[cx.bx[tc]])
            if ih == 1:
                ln_stats_tile(cx, tc)





def build_l1_ffn(cx, H):
    nc, p = cx.nc, cx.p
    p.fence()
    for a in (cx.BIG1, cx.BIG2, cx.STRIPE, cx.WT, cx.SCR):
        a.reset()
    Q = float(1.0 / 127.0)

    ln4 = cx.BIG1.take(BF16, 128, ND * S); b_ln4 = Buf("ln4")
    layernorm(cx, ln4, b_ln4, pre_stats=True)

    h1c = [cx.BIG2.take(BF16, 128, S) for _ in range(8)] + \
          [cx.STRIPE.take(BF16, 128, S) for _ in range(4)]
    b_h1 = [Buf(f"h1_{i}") for i in range(12)]

    wg = [cx.WT.take(BF16, 128, ND * 128) for _ in range(2)]
    wu = [cx.WT.take(BF16, 128, ND * 128) for _ in range(2)]
    wd = [cx.WT.take(BF16, 128, 1024) for _ in range(12)]
    b_wg = [Buf("l1f_wg0"), Buf("l1f_wg1")]
    b_wu = [Buf("l1f_wu0"), Buf("l1f_wu1")]
    b_wd = [Buf(f"l1f_wd{i}") for i in range(12)]
    sg = cx.SCR.take(BF16, 128, 1024); b_sg = Buf("l1f_sg")

    def load_gu(ic, slot):
        for (t_, b_, name) in ((wg[slot], b_wg[slot], "l1f_wgT"),
                               (wu[slot], b_wu[slot], "l1f_wuT")):
            src_ = H[name][:, ic * 128:(ic + 1) * 128].rearrange("(c p) i -> p c i", p=128)
            p.do("gpsimd", lambda g, t_=t_, src_=src_: g.dma_start(
                out=t_.rearrange("p (c i) -> p c i", i=128), in_=src_),
                writes=[b_], dma=True)

    def load_wd(ic, icl):
        src_ = H["l1f_wdT"][ic * 128:(ic + 1) * 128, :]
        p.do("gpsimd", lambda g, icl=icl, src_=src_: g.dma_start(
            out=wd[icl][:, :], in_=src_), writes=[b_wd[icl]], dma=True)

    load_gu(0, 0)
    for half in range(2):
        for icl in range(12):
            ic = half * 12 + icl
            slot = ic % 2
            if ic + 1 < 24:
                load_gu(ic + 1, 1 - slot)
            load_wd(ic, icl)
            for tt in range(4):
                pbg = tt % 2
                pbu = 2 + tt % 2
                pg = cx.psum[pbg].ap()[:, 0:512]
                pu = cx.psum[pbu].ap()[:, 0:512]
                for kc in range(ND):
                    p.do("tensor", lambda t, tt=tt, kc=kc, pg=pg, slot=slot: t.matmul(
                        pg, wg[slot][:, kc * 128:(kc + 1) * 128], fma(ln4, kc, tt * 512, 512),
                        start=(kc == 0), stop=(kc == ND - 1)),
                        reads=[b_wg[slot], b_ln4], writes=[cx.b_psum[pbg]])
                    p.do("tensor", lambda t, tt=tt, kc=kc, pu=pu, slot=slot: t.matmul(
                        pu, wu[slot][:, kc * 128:(kc + 1) * 128], fma(ln4, kc, tt * 512, 512),
                        start=(kc == 0), stop=(kc == ND - 1)),
                        reads=[b_wu[slot], b_ln4], writes=[cx.b_psum[pbu]])
                sgh = sg[:, (tt % 2) * 512:(tt % 2) * 512 + 512]
                p.do("scalar", lambda s, sgh=sgh, pg=pg: s.activation(
                    out=sgh, in_=pg, func=AF.Silu, scale=Q),
                    reads=[cx.b_psum[pbg]], writes=[b_sg])
                p.do("vector", lambda v, icl=icl, tt=tt, sgh=sgh, pu=pu: v.scalar_tensor_tensor(
                    out=h1c[icl][:, tt * 512:(tt + 1) * 512], in0=pu, scalar=Q, in1=sgh,
                    op0=ALU.mult, op1=ALU.mult),
                    reads=[cx.b_psum[pbu], b_sg], writes=[b_h1[icl]])
        for tc in range(NT):
            for dc in range(2):
                pbd = 4 + (tc * 2 + dc) % 2
                pd = cx.psum[pbd].ap()[:, 0:512]
                for icl in range(12):
                    p.do("tensor", lambda t, tc=tc, dc=dc, icl=icl, pd=pd: t.matmul(
                        pd, h1c[icl][:, tc * 128:(tc + 1) * 128],
                        wd[icl][:, dc * 512:(dc + 1) * 512],
                        start=(icl == 0), stop=(icl == 11)),
                        reads=[b_h1[icl], b_wd[icl]], writes=[cx.b_psum[pbd]])
                p.do("vector", lambda v, tc=tc, dc=dc, pd=pd: v.scalar_tensor_tensor(
                    out=cx.xt[tc][:, dc * 512:(dc + 1) * 512], in0=pd, scalar=Q,
                    in1=cx.xt[tc][:, dc * 512:(dc + 1) * 512], op0=ALU.mult, op1=ALU.add),
                    reads=[cx.b_psum[pbd]], writes=[cx.bx[tc]])



HBM_SPECS = {
    "L0_WqT": ([D, D], np.float32), "L0_WkT": ([D, D], np.float32),
    "L0_WvT": ([D, D], np.float32), "L0_WoT": ([D, D], np.float32),
    "L0_bq_c": ([128, 8], np.float32), "L0_bk_c": ([128, 8], np.float32),
    "L0_bv_c": ([128, 8], np.float32), "L0_bo_r": ([1, D], np.float32),
    "L0_projT": ([16, 64, 64], np.float32),
    "l0f_gwT": ([D, 8], np.float32), "l0f_lb": ([1, 8], np.float32),
    "l0f_egwT": ([8, D, 512], np.float32), "l0f_euwT": ([8, D, 512], np.float32),
    "l0f_edwT": ([8, 512, D], np.float32),
    "l0f_sgwT": ([D, D], np.float32), "l0f_suwT": ([D, D], np.float32),
    "l0f_sdwT": ([D, D], np.float32),
    "l1_qkvw": ([D, 3 * D], np.float32), "l1_qkvb_c": ([128, 24], np.float32),
    "l1_rqw": ([D, 256], np.float32), "l1_rqb_c": ([128, 2], np.float32),
    "l1_rkvw": ([D, 512], np.float32), "l1_rkvb_c": ([128, 4], np.float32),
    "l1_routw": ([256, D], np.float32), "l1_routb_c": ([128, 8], np.float32),
    "l1_outw": ([D, D], np.float32), "l1_outb_r": ([1, D], np.float32),
    "l1_gl_r": ([1, 128], np.float32), "l1_gr_r": ([1, 128], np.float32),
    "l1f_wgT": ([D, 3 * D], np.int32), "l1f_wuT": ([D, 3 * D], np.int32),
    "l1f_wdT": ([3 * D, D], np.int32),
}


def _prep_weights(inputs):
    c = np.ascontiguousarray
    H = {}
    H["L0_WqT"] = c(inputs["L0_attn_Wq"].T)
    H["L0_WkT"] = c(inputs["L0_attn_Wk"].T)
    H["L0_WvT"] = c(inputs["L0_attn_Wv"].T)
    H["L0_WoT"] = c(inputs["L0_attn_Wo"].T)
    H["L0_bq_c"] = c(inputs["L0_attn_bq"].reshape(8, 128).T)
    H["L0_bk_c"] = c(inputs["L0_attn_bk"].reshape(8, 128).T)
    H["L0_bv_c"] = c(inputs["L0_attn_bv"].reshape(8, 128).T)
    H["L0_bo_r"] = c(inputs["L0_attn_bo"].reshape(1, D))
    H["L0_projT"] = c(inputs["L0_attn_proj"].transpose(0, 2, 1))
    H["l0f_gwT"] = c(inputs["L0_ffn_gate_w"].T)
    H["l0f_lb"] = c(inputs["L0_ffn_lb_bias"].reshape(1, 8))
    H["l0f_egwT"] = c(inputs["L0_ffn_expert_gate_w"].transpose(0, 2, 1))
    H["l0f_euwT"] = c(inputs["L0_ffn_expert_up_w"].transpose(0, 2, 1))
    H["l0f_edwT"] = c(inputs["L0_ffn_expert_down_w"].transpose(0, 2, 1))
    H["l0f_sgwT"] = c(inputs["L0_ffn_shared_gate_w"].T)
    H["l0f_suwT"] = c(inputs["L0_ffn_shared_up_w"].T)
    H["l0f_sdwT"] = c(inputs["L0_ffn_shared_down_w"].T)
    H["l1_qkvw"] = c(inputs["L1_attn_qkv_w"])
    H["l1_qkvb_c"] = c(inputs["L1_attn_qkv_b"].reshape(24, 128).T)
    H["l1_rqw"] = c(inputs["L1_attn_rq_w"])
    H["l1_rqb_c"] = c(inputs["L1_attn_rq_b"].reshape(2, 128).T)
    H["l1_rkvw"] = c(inputs["L1_attn_rkv_w"])
    H["l1_rkvb_c"] = c(inputs["L1_attn_rkv_b"].reshape(4, 128).T)
    H["l1_routw"] = c(inputs["L1_attn_rout_w"])
    H["l1_routb_c"] = c(inputs["L1_attn_rout_b"].reshape(8, 128).T)
    H["l1_outw"] = c(inputs["L1_attn_out_w"])
    H["l1_outb_r"] = c(inputs["L1_attn_out_b"].reshape(1, D))
    H["l1_gl_r"] = c(np.tile(inputs["L1_attn_gl_w"].reshape(1, 64), (1, 2)))
    H["l1_gr_r"] = c(np.tile(inputs["L1_attn_gr_w"].reshape(1, 64), (1, 2)))
    H["l1f_wgT"] = c(inputs["L1_ffn_w_gate"].T)
    H["l1f_wuT"] = c(inputs["L1_ffn_w_up"].T)
    H["l1f_wdT"] = c(inputs["L1_ffn_w_down"].T)
    return H


BLOCKS = ("l0a", "l0f", "l1a", "l1f")
DBG = False


def _build_program(blocks):
    nc = bass.Bass("TRN2", target_bir_lowering=False)
    xin = nc.dram_tensor("x", [S, D], F32, kind="ExternalInput").ap()
    xout = nc.dram_tensor("xo", [S, D], F32, kind="ExternalOutput").ap()
    npdt = {np.float32: F32, np.int32: I32}
    Hb = {k: nc.dram_tensor(k, list(shape), npdt[dt], kind="ExternalInput").ap()
          for k, (shape, dt) in HBM_SPECS.items()}
    p = Prog(nc)
    globals()["LASTPROG"] = p
    cx = build_context(nc, p)
    load_x(cx, xin)
    if "l0a" in blocks:
        build_l0_attn(cx, Hb)
    if "l0f" in blocks:
        build_l0_ffn(cx, Hb)
    if "l1a" in blocks:
        build_l1_attn(cx, Hb)
    if "l1f" in blocks:
        build_l1_ffn(cx, Hb)
    final = store_x(cx, xout)
    p.finalize_and_emit(final)
    return nc


def kernel(**inputs):
    H = _prep_weights(inputs)
    x = np.ascontiguousarray(inputs["L0_attn_query"].astype(np.float32, copy=False))
    nc = _build_program(BLOCKS)
    in_maps = []
    for core in range(8):
        m = {"x": np.ascontiguousarray(x[core])}
        m.update(H)
        in_maps.append(m)
    res = run_bass_kernel_spmd(nc, in_maps, core_ids=list(range(8)))
    return np.stack([res.results[i]["xo"] for i in range(8)], axis=0)


# version 20
# speedup vs baseline: 1.0211x; 1.0211x over previous
import numpy as np
import concourse.bass as bass
import concourse.mybir as mybir
from concourse.bass_utils import run_bass_kernel_spmd

F32 = mybir.dt.float32
BF16 = mybir.dt.bfloat16
I32 = mybir.dt.int32
AF = mybir.ActivationFunctionType
ALU = mybir.AluOpType
AX = mybir.AxisListType

S = 2048
D = 1024
NT = S // 128
ND = D // 128
LNR = float(np.log(1.0 / 8.0))
NRM = float(64.0 ** -0.25)

ENGS = ("tensor", "vector", "scalar", "gpsimd", "sync")
NDMASLOT = {"sync": 6, "gpsimd": 10, "scalar": 4}



class Op:
    __slots__ = ("eng", "idx", "builder", "deps", "dma", "slot", "slotval", "sig",
                 "seq", "dma_waits")

    def __init__(self, eng, idx, builder, deps, dma, slot, slotval):
        self.eng, self.idx, self.builder = eng, idx, builder
        self.deps, self.dma, self.slot, self.slotval = deps, dma, slot, slotval
        self.sig = dma
        self.seq = None
        self.dma_waits = {}


class Buf:
    __slots__ = ("ws", "rs", "name")

    def __init__(self, name=""):
        self.ws, self.rs, self.name = [], [], name


class Prog:
    def __init__(self, nc):
        self.nc = nc
        self.ops = []
        self.streams = {e: [] for e in ENGS}
        self.dma_next = {e: 0 for e in NDMASLOT}
        self.dma_cnt = {}
        self.fence_ops = []
        self.fence_dma = {}

    def fence(self):
        self.fence_ops = [s[-1] for s in self.streams.values() if s]
        self.fence_dma = {slot: 16 * cnt for slot, cnt in self.dma_cnt.items()}

    def do(self, eng, builder, reads=(), writes=(), deps=(), dma=False):
        dep_ops = {}
        for b in reads:
            for w in b.ws:
                dep_ops[w.idx] = w
        for b in writes:
            for w in b.ws:
                dep_ops[w.idx] = w
            for r in b.rs:
                dep_ops[r.idx] = r
        for d in deps:
            if d is not None:
                dep_ops[d.idx] = d
        for f in self.fence_ops:
            dep_ops[f.idx] = f
        dma_waits = dict(self.fence_dma)
        for d in dep_ops.values():
            if d.dma:
                dma_waits[d.slot] = 16 * self.dma_cnt[d.slot]
        slot = slotval = None
        if dma:
            sl = self.dma_next[eng]
            self.dma_next[eng] = (sl + 1) % NDMASLOT[eng]
            slot = (eng, sl)
            prior = 16 * self.dma_cnt.get(slot, 0)
            if prior > dma_waits.get(slot, 0):
                dma_waits[slot] = prior
            self.dma_cnt[slot] = self.dma_cnt.get(slot, 0) + 1
            slotval = 16 * self.dma_cnt[slot]
        op = Op(eng, len(self.ops), builder,
                [dep_ops[i] for i in sorted(dep_ops)], dma, slot, slotval)
        op.dma_waits = dma_waits
        self.ops.append(op)
        self.streams[eng].append(op)
        for b in reads:
            b.rs.append(op)
        for b in writes:
            b.ws = [op]
            b.rs = []
        for d in op.deps:
            if d.dma:
                continue
            if d.eng == "tensor" and eng == "tensor":
                continue
            d.sig = True
        return op

    def finalize_and_emit(self, final_dma_ops):
        nc = self.nc
        cnt = {e: 0 for e in ENGS}
        for op in self.ops:
            if op.sig and not op.dma:
                cnt[op.eng] += 1
                op.seq = cnt[op.eng]
        csem = {e: nc.alloc_semaphore(name=f"c_{e}") for e in ENGS}
        dsem = {}
        for e, n in NDMASLOT.items():
            for sl in range(n):
                dsem[(e, sl)] = nc.alloc_semaphore(name=f"d_{e}{sl}")
        streams = self.streams

        def emit_stream(eng_name):
            def run(engine):
                wm = {}
                for op in streams[eng_name]:
                    for slot, val in sorted(op.dma_waits.items()):
                        if wm.get(slot, 0) < val:
                            wm[slot] = val
                            engine.wait_ge(dsem[slot], val)
                    for d in op.deps:
                        if d.dma:
                            continue
                        if d.eng == "tensor" and eng_name == "tensor":
                            continue
                        key, val, sem = d.eng, d.seq, csem[d.eng]
                        if wm.get(key, 0) < val:
                            wm[key] = val
                            engine.wait_ge(sem, val)
                    inst = op.builder(engine)
                    if op.dma:
                        inst.then_inc(dsem[op.slot], 16)
                    elif op.sig:
                        inst.then_inc(csem[op.eng], 1)
                if eng_name == "sync":
                    fin = {}
                    for d in final_dma_ops:
                        fin[d.slot] = max(fin.get(d.slot, 0), d.slotval)
                    for slot, val in sorted(fin.items()):
                        engine.wait_ge(dsem[slot], val)
            return run

        with nc.Block() as block:
            block.tensor(emit_stream("tensor"))
            block.vector(emit_stream("vector"))
            block.scalar(emit_stream("scalar"))
            block.gpsimd(emit_stream("gpsimd"))
            block.sync(emit_stream("sync"))


class Arena:
    def __init__(self, nc, name, nbytes):
        self.t = nc.alloc_sbuf_tensor(name, [128, nbytes], mybir.dt.uint8)
        self.nbytes = nbytes
        self.off = 0
        self.name = name

    def take(self, dtype, rows, cols, align=32):
        esz = mybir.dt.size(dtype)
        off = (self.off + align - 1) // align * align
        need = cols * esz
        assert off + need <= self.nbytes, \
            f"arena {self.name} overflow: {off}+{need} > {self.nbytes}"
        self.off = off + need
        return self.at(off, dtype, rows, cols)

    def at(self, off, dtype, rows, cols):
        esz = mybir.dt.size(dtype)
        assert off + cols * esz <= self.nbytes
        v = self.t.ap()[:, off:off + cols * esz].bitcast(dtype)
        if rows != 128:
            v = v[0:rows, :]
        return v

    def reset(self, off=0):
        self.off = off


class Ctx:
    pass


def fma(fmbuf, c, t0, tn):
    return fmbuf[:, c * S + t0: c * S + t0 + tn]


def build_context(nc, p):
    cx = Ctx()
    cx.nc, cx.p = nc, p
    cx.X = Arena(nc, "AX", 65536)
    cx.CONST = Arena(nc, "ACONST", 2880)
    cx.BIG1 = Arena(nc, "ABIG1", 32768)
    cx.BIG2 = Arena(nc, "ABIG2", 32768)
    cx.STRIPE = Arena(nc, "ASTRIPE", 20480)
    cx.WT = Arena(nc, "AWT", 40960)
    cx.SCR = Arena(nc, "ASCR", 9216)
    cx.SCR2 = Arena(nc, "ASCR2", 8192)

    cx.xt = [cx.X.take(F32, 128, 1024) for _ in range(NT)]
    cx.bx = [Buf(f"x{t}") for t in range(NT)]

    cx.psum = [nc.alloc_psum_tensor(f"pb{i}", [128, 512], F32) for i in range(6)]
    cx.psumT = [nc.alloc_psum_tensor(f"pt{i}", [128, 1024], BF16) for i in range(2)]
    cx.b_psum = [Buf(f"pb{i}") for i in range(6)]
    cx.b_psumT = [Buf(f"pt{i}") for i in range(2)]

    cx.ident_bf = cx.CONST.take(BF16, 128, 128)
    cx.ident_f = cx.CONST.take(F32, 128, 128)
    cx.ones_row = cx.CONST.take(BF16, 1, 512)
    cx.ones_col = cx.CONST.take(BF16, 128, 1)
    cx.onesf_row = cx.CONST.take(F32, 1, 128)
    cx.onesf2 = cx.CONST.take(F32, 2, 128)
    cx.b_ident_bf, cx.b_ident_f = Buf("idb"), Buf("idf")
    cx.b_ones_row, cx.b_ones_col, cx.b_onesf = Buf("or"), Buf("oc"), Buf("ofr")
    ib, idf = cx.ident_bf, cx.ident_f
    p.do("gpsimd", lambda g: g.memset(ib, 1.0), writes=[cx.b_ident_bf])
    p.do("gpsimd", lambda g: g.affine_select(ib, ib, [[1, 128]], ALU.is_equal, 0.0,
                                             base=0, channel_multiplier=-1),
         writes=[cx.b_ident_bf])
    p.do("gpsimd", lambda g: g.memset(idf, 1.0), writes=[cx.b_ident_f])
    p.do("gpsimd", lambda g: g.affine_select(idf, idf, [[1, 128]], ALU.is_equal, 0.0,
                                             base=0, channel_multiplier=-1),
         writes=[cx.b_ident_f])
    p.do("gpsimd", lambda g: g.memset(cx.ones_row, 1.0), writes=[cx.b_ones_row])
    p.do("gpsimd", lambda g: g.memset(cx.ones_col, 1.0), writes=[cx.b_ones_col])
    p.do("gpsimd", lambda g: g.memset(cx.onesf_row, 1.0), writes=[cx.b_onesf])
    p.do("gpsimd", lambda g: g.memset(cx.onesf2, 1.0), writes=[cx.b_onesf])
    cx.b_lnst, cx.b_lnd = Buf("lnst"), Buf("lnd")
    cx.ccn = 0
    return cx


def load_x(cx, xin):
    p = cx.p
    for tc in range(NT):
        p.do("sync", lambda s, tc=tc: s.dma_start(
            out=cx.xt[tc], in_=xin[tc * 128:(tc + 1) * 128, :]),
            writes=[cx.bx[tc]], dma=True)
        ln_stats_tile(cx, tc)


def store_x(cx, xout):
    p = cx.p
    return [p.do("sync", lambda s, tc=tc: s.dma_start(
        out=xout[tc * 128:(tc + 1) * 128, :], in_=cx.xt[tc]),
        reads=[cx.bx[tc]], dma=True) for tc in range(NT)]


def transpose_bf(cx, src_ap, dst_ap, b_src, b_dst, pb):
    p = cx.p
    pv = cx.psumT[pb].ap()[:, 0:128]
    p.do("tensor", lambda t, src_ap=src_ap, pv=pv: t.transpose(pv, src_ap, cx.ident_bf),
         reads=[b_src, cx.b_ident_bf], writes=[cx.b_psumT[pb]])
    p.do("vector", lambda v, dst_ap=dst_ap, pv=pv: v.tensor_copy(out=dst_ap, in_=pv),
         reads=[cx.b_psumT[pb]], writes=[b_dst])


def transposeN_bf(cx, srcs, b_srcs, dst_ap, b_dst, pb):
    p = cx.p
    n = len(srcs)
    bank = cx.psumT[pb]
    for i, s in enumerate(srcs):
        pv = bank.ap()[:, i * 128:(i + 1) * 128]
        p.do("tensor", lambda t, s=s, pv=pv: t.transpose(pv, s, cx.ident_bf),
             reads=[b_srcs[i], cx.b_ident_bf], writes=[cx.b_psumT[pb]])
    pvall = bank.ap()[:, 0:n * 128]
    p.do("vector", lambda v, dst_ap=dst_ap, pvall=pvall: v.tensor_copy(
        out=dst_ap, in_=pvall),
        reads=[cx.b_psumT[pb]], writes=[b_dst])



def ln_stats_tile(cx, tc):
    p = cx.p
    A = cx.SCR2
    ssum = A.at(0, F32, 128, NT)
    ssq = A.at(64, F32, 128, NT)
    lnd = A.at(320, BF16, 128, 1024)
    xa = cx.xt[tc]
    p.do("vector", lambda v, xa=xa, tc=tc: v.reduce_sum(
        out=ssum[:, tc:tc + 1], in_=xa, axis=AX.X), reads=[cx.bx[tc]],
        writes=[cx.b_lnst])
    p.do("scalar", lambda s, xa=xa, tc=tc: s.activation(
        out=lnd, in_=xa, func=AF.Square, accum_out=ssq[:, tc:tc + 1]),
        reads=[cx.bx[tc]], writes=[cx.b_lnd, cx.b_lnst])


def layernorm(cx, out_fm, b_out, per_tile_cb=None, stats_out=None, pre_stats=False):
    p = cx.p
    A = cx.SCR2
    ssum = A.at(0, F32, 128, NT)
    ssq = A.at(64, F32, 128, NT)
    smean = A.at(128, F32, 128, NT)
    srstd = A.at(192, F32, 128, NT)
    svar = A.at(256, F32, 128, NT)
    lnd = A.at(320, BF16, 128, 1024)
    b_st, b_lnd = cx.b_lnst, cx.b_lnd
    if not pre_stats:
        for tc in range(NT):
            ln_stats_tile(cx, tc)
    p.do("vector", lambda v: v.tensor_scalar(out=smean, in0=ssum, scalar1=1.0 / D,
                                             scalar2=None, op0=ALU.mult),
         reads=[b_st], writes=[b_st])
    p.do("vector", lambda v: v.tensor_mul(out=svar, in0=smean, in1=smean),
         reads=[b_st], writes=[b_st])
    p.do("vector", lambda v: v.scalar_tensor_tensor(out=svar, in0=ssq, scalar=1.0 / D,
                                                    in1=svar, op0=ALU.mult, op1=ALU.subtract),
         reads=[b_st], writes=[b_st])
    p.do("vector", lambda v: v.tensor_scalar(out=svar, in0=svar, scalar1=1e-5,
                                             scalar2=None, op0=ALU.add),
         reads=[b_st], writes=[b_st])
    p.do("scalar", lambda s: s.activation(out=svar, in_=svar, func=AF.Sqrt),
         reads=[b_st], writes=[b_st])
    p.do("vector", lambda v: v.reciprocal(out=srstd, in_=svar),
         reads=[b_st], writes=[b_st])
    if stats_out is not None:
        stats_out[0], stats_out[1], stats_out[2] = smean, srstd, b_st
    for tc in range(NT):
        xa = cx.xt[tc]
        p.do("vector", lambda v, xa=xa, tc=tc: v.tensor_scalar(
            out=lnd, in0=xa, scalar1=smean[:, tc:tc + 1], scalar2=srstd[:, tc:tc + 1],
            op0=ALU.subtract, op1=ALU.mult), reads=[cx.bx[tc], b_st], writes=[b_lnd])
        dstv = out_fm[:, tc * 128: (ND - 1) * S + tc * 128 + 128].rearrange(
            "p (c i) -> p c i", c=ND) if False else None
        import concourse.bass as _b
        dv = out_fm.rearrange("p (c i) -> p c i", c=ND)[:, :, tc * 128: tc * 128 + 128]
        transposeN_bf(cx, [lnd[:, c * 128:(c + 1) * 128] for c in range(ND)],
                      [b_lnd] * ND, dv, b_out, pb=tc % 2)
        if per_tile_cb is not None:
            per_tile_cb(tc, lnd, b_lnd)



def build_l0_attn(cx, H):
    nc, p = cx.nc, cx.p
    p.fence()
    for a in (cx.BIG1, cx.BIG2, cx.STRIPE, cx.WT, cx.SCR):
        a.reset()

    ln1 = cx.BIG1.take(BF16, 128, ND * S); b_ln1 = Buf("ln1")
    att = cx.BIG2.take(BF16, 128, ND * S); b_att = Buf("att")
    qf2 = [cx.STRIPE.take(BF16, 128, S), cx.WT.take(BF16, 128, S)]
    b_qf2 = [Buf("qf0"), Buf("qf1")]
    kf2 = [cx.STRIPE.take(BF16, 128, S), cx.WT.take(BF16, 128, S)]
    b_kf2 = [Buf("kf0"), Buf("kf1")]
    vt2 = [cx.STRIPE.take(BF16, 128, S), cx.WT.take(BF16, 128, S)]
    b_vt2 = [Buf("vt0"), Buf("vt1")]
    qp = cx.STRIPE.take(BF16, 128, S); b_qp = Buf("qp")
    kp = cx.STRIPE.take(BF16, 128, S); b_kp = Buf("kp")

    wsl = [cx.WT.take(BF16, 128, ND * 128) for _ in range(6)]
    b_wsl = [Buf(f"w{i}") for i in range(6)]
    wo = cx.WT.take(BF16, 128, ND * D); b_wo = Buf("wo")

    A = cx.SCR
    bqc = A.take(F32, 128, ND); b_bqc = Buf("bqc")
    bkc = A.take(F32, 128, ND); b_bkc = Buf("bkc")
    bvc = A.take(F32, 128, ND); b_bvc = Buf("bvc")
    bq_s = A.take(F32, 128, ND); b_bqs = Buf("bqs")
    bk_s = A.take(F32, 128, ND); b_bks = Buf("bks")
    bo_row = A.take(BF16, 1, D); b_bor = Buf("bor")
    projbd = A.take(BF16, 128, 8 * 128); b_projbd = Buf("projbd")
    halfbd = A.take(BF16, 128, 2); b_hbd = Buf("halfbd")
    dq_tm = A.take(F32, 128, 2 * NT); b_dqt = Buf("dqt")
    dk_tm = A.take(F32, 128, 2 * NT); b_dkt = Buf("dkt")
    mxk = A.take(F32, 128, 2 * NT); b_mxk = Buf("mxk")
    stq_all = A.take(F32, 128, 2 * NT); b_stq = Buf("stq_all")
    biasq = A.take(F32, 128, 2 * NT); b_biasq = Buf("biasq")
    biask = A.take(F32, 128, 2 * NT); b_biask = Buf("biask")
    nrm_all = A.take(F32, 128, 2 * NT); b_nrm = Buf("nrm_all")
    mxr = A.take(F32, 128, 2); b_mxr = Buf("mxr")
    stabkb = A.take(F32, 128, 2); b_stabkb = Buf("stabkb")
    kvblk = A.take(BF16, 128, 132); b_kvblk = Buf("kvblk")
    ksrow = A.take(F32, 1, 128); b_ksrow = Buf("ksrow")
    ksrb = A.take(F32, 128, 128); b_ksrb = Buf("ksrb")
    ksr2 = A.take(F32, 2, 128); b_ksr2 = Buf("ksr2")
    q2 = cx.SCR2.at(4096, BF16, 128, 2048); b_q2 = Buf("q2")
    qsc = A.take(BF16, 128, 1024); b_qsc = Buf("qsc")

    def th_bank_view(ap_, g):
        return ap_.rearrange("p (q g h) -> p g q h", q=4, g=4)[:, g:g + 1].squeeze(1)

    def ht_bank_view(ap_, g):
        return ap_.rearrange("p (h q g) -> p g q h", h=2, q=4)[:, g:g + 1].squeeze(1)

    def stripe_bank_view(ap_, g):
        return ap_.rearrange("p (q g h j) -> p g q h j", q=4, g=4, h=2)[
            :, g:g + 1].squeeze(1)

    def load_pair_w(f, slot):
        for j, name in enumerate(("L0_WqT", "L0_WkT", "L0_WvT")):
            src = H[name][:, f * 128:(f + 1) * 128].rearrange("(c p) i -> p c i", p=128)
            t_, b_ = wsl[slot * 3 + j], b_wsl[slot * 3 + j]
            dst = t_.rearrange("p (c i) -> p c i", i=128)
            p.do("gpsimd", lambda g, dst=dst, src=src: g.dma_start(out=dst, in_=src),
                 writes=[b_], dma=True)

    load_pair_w(0, 0)
    p.do("sync", lambda s: s.dma_start(out=bqc, in_=H["L0_bq_c"]), writes=[b_bqc], dma=True)
    p.do("sync", lambda s: s.dma_start(out=bkc, in_=H["L0_bk_c"]), writes=[b_bkc], dma=True)
    p.do("sync", lambda s: s.dma_start(out=bvc, in_=H["L0_bv_c"]), writes=[b_bvc], dma=True)
    p.do("gpsimd", lambda g: g.dma_start(out=bo_row, in_=H["L0_bo_r"]), writes=[b_bor], dma=True)
    p.do("gpsimd", lambda g: g.memset(projbd, 0.0), writes=[b_projbd])
    for f in range(8):
        pa = H["L0_projT"]
        p.do("gpsimd", lambda g, f=f, pa=pa: g.dma_start(
            out=projbd[0:64, f * 128: f * 128 + 64], in_=pa[2 * f]), writes=[b_projbd], dma=True)
        p.do("gpsimd", lambda g, f=f, pa=pa: g.dma_start(
            out=projbd[64:128, f * 128 + 64: f * 128 + 128], in_=pa[2 * f + 1]),
            writes=[b_projbd], dma=True)
    p.do("gpsimd", lambda g: g.memset(halfbd, 0.0), writes=[b_hbd])
    p.do("gpsimd", lambda g: g.memset(halfbd[0:64, 0:1], 0.5), writes=[b_hbd])
    p.do("gpsimd", lambda g: g.memset(halfbd[64:128, 1:2], 0.5), writes=[b_hbd])
    p.do("vector", lambda v: v.tensor_scalar(out=bq_s, in0=bqc, scalar1=NRM, scalar2=None,
                                             op0=ALU.mult), reads=[b_bqc], writes=[b_bqs])
    p.do("vector", lambda v: v.tensor_scalar(out=bk_s, in0=bkc, scalar1=NRM, scalar2=None,
                                             op0=ALU.mult), reads=[b_bkc], writes=[b_bks])
    p.do("gpsimd", lambda g: g.dma_start(
        out=wo.rearrange("p (c i) -> p c i", i=D),
        in_=H["L0_WoT"].rearrange("(c p) i -> p c i", p=128)),
        writes=[b_wo], dma=True)

    layernorm(cx, ln1, b_ln1, pre_stats=True)

    def l0_proj(f, which):
        slot = f % 2
        wq_, wk_, wv_ = wsl[slot * 3], wsl[slot * 3 + 1], wsl[slot * 3 + 2]
        b_wq_, b_wk_, b_wv_ = b_wsl[slot * 3], b_wsl[slot * 3 + 1], b_wsl[slot * 3 + 2]
        qf, b_qf = qf2[f % 2], b_qf2[f % 2]
        kf, b_kf = kf2[f % 2], b_kf2[f % 2]
        vt, b_vt = vt2[f % 2], b_vt2[f % 2]
        specs = {"qk": ((wq_, b_wq_, qf, b_qf, bq_s, b_bqs, NRM),
                        (wk_, b_wk_, kf, b_kf, bk_s, b_bks, NRM)),
                 "v": ((wv_, b_wv_, q2, b_q2, bvc, b_bvc, 1.0),)}
        for (w_, b_w_, dst, b_dst, bias, b_bias, scale) in specs[which]:
            for tt in range(4):
                pb = 4 + tt % 2
                pv = cx.psum[pb].ap()[:, 0:512]
                for kc in range(ND):
                    p.do("tensor", lambda t, w_=w_, kc=kc, tt=tt, pv=pv: t.matmul(
                        pv, w_[:, kc * 128:(kc + 1) * 128], fma(ln1, kc, tt * 512, 512),
                        start=(kc == 0), stop=(kc == ND - 1)),
                        reads=[b_w_, b_ln1], writes=[cx.b_psum[pb]])
                p.do("scalar", lambda s, dst=dst, tt=tt, pv=pv, bias=bias, f=f, scale=scale:
                     s.activation(out=dst[:, tt * 512:(tt + 1) * 512], in_=pv, func=AF.Identity,
                                  bias=bias[:, f:f + 1], scale=scale),
                     reads=[cx.b_psum[pb], b_bias], writes=[b_dst])
        if which == "v":
            for tg in range(2):
                transposeN_bf(cx, [q2[:, tc * 128:(tc + 1) * 128]
                                   for tc in range(tg * 8, tg * 8 + 8)],
                              [b_q2] * 8, vt[:, tg * 1024:(tg + 1) * 1024], b_vt,
                              pb=tg % 2)

    l0_proj(0, "qk")
    l0_proj(0, "v")

    for f in range(8):
        if f < 7:
            load_pair_w(f + 1, (f + 1) % 2)
        qf, b_qf = qf2[f % 2], b_qf2[f % 2]
        kf, b_kf = kf2[f % 2], b_kf2[f % 2]
        vt, b_vt = vt2[f % 2], b_vt2[f % 2]

        for (src, b_src, dst, b_dst) in ((qf, b_qf, dq_tm, b_dqt), (kf, b_kf, dk_tm, b_dkt)):
            p.do("scalar", lambda s, src=src: s.activation(out=q2, in_=src, func=AF.Square),
                 reads=[b_src], writes=[b_q2])
            pv = cx.psum[4].ap()[:, 0:32]
            for tc in range(NT):
                p.do("tensor", lambda t, tc=tc, pv=pv: t.matmul(
                    pv[:, 2 * tc:2 * tc + 2], q2[:, tc * 128:(tc + 1) * 128], halfbd,
                    start=(tc == 0), stop=(tc == NT - 1)),
                    reads=[b_q2, b_hbd], writes=[cx.b_psum[4]])
            p.do("vector", lambda v, dst=dst, pv=pv: v.tensor_copy(out=dst, in_=pv),
                 reads=[cx.b_psum[4]], writes=[b_dst])

        for tc in range(NT):
            pb = tc % 4
            pv = cx.psum[pb].ap()[:, (tc // 4) * 128:(tc // 4) * 128 + 128]
            p.do("tensor", lambda t, tc=tc, pv=pv, f=f, kf=kf: t.matmul(
                pv, kf[:, tc * 128:(tc + 1) * 128], projbd[:, f * 128:(f + 1) * 128],
                start=(tc < 4), stop=(tc >= NT - 4)),
                reads=[b_kf, b_projbd], writes=[cx.b_psum[pb]])
        for g in range(4):
            pv4d = cx.psum[g].ap().rearrange("p (q h j) -> p q h j", h=2, j=64)
            p.do("vector", lambda v, g=g, pv4d=pv4d: v.reduce_max(
                out=ht_bank_view(mxk, g), in_=pv4d, axis=AX.X),
                reads=[cx.b_psum[g]], writes=[b_mxk])
        p.do("vector", lambda v: v.reduce_max(
            out=mxr, in_=mxk.rearrange("p (h t) -> p h t", h=2), axis=AX.X),
            reads=[b_mxk], writes=[b_mxr])
        pv5 = cx.psum[5].ap()
        p.do("tensor", lambda t: t.transpose(pv5[0:2, 256:384], mxr, cx.ident_f),
             reads=[b_mxr, cx.b_ident_f], writes=[cx.b_psum[5]])
        p.do("vector", lambda v: v.reduce_max(out=ksr2[0:2, 0:1], in_=pv5[0:2, 256:384], axis=AX.X),
             reads=[cx.b_psum[5]], writes=[b_ksr2])
        p.do("tensor", lambda t: t.transpose(pv5[0:1, 384:386], ksr2[0:2, 0:1],
                                             cx.ident_f[0:2, 0:2]),
             reads=[b_ksr2, cx.b_ident_f], writes=[cx.b_psum[5]])
        p.do("vector", lambda v: v.tensor_copy(out=ksrow[0:1, 0:2], in_=pv5[0:1, 384:386]),
             reads=[cx.b_psum[5]], writes=[b_ksrow])
        p.do("tensor", lambda t: t.matmul(pv5[:, 0:2], cx.onesf_row, ksrow[0:1, 0:2],
                                          start=True, stop=True),
             reads=[cx.b_onesf, b_ksrow], writes=[cx.b_psum[5]])
        p.do("vector", lambda v: v.tensor_copy(out=stabkb, in_=pv5[:, 0:2]),
             reads=[cx.b_psum[5]], writes=[b_stabkb])

        p.do("gpsimd", lambda v: v.tensor_scalar(
            out=biask, in0=dk_tm, scalar1=-1.0, scalar2=LNR,
            op0=ALU.mult, op1=ALU.add), reads=[b_dkt], writes=[b_biask])
        p.do("gpsimd", lambda v: v.tensor_tensor(
            out=biask.rearrange("p (t h) -> p t h", h=2),
            in0=biask.rearrange("p (t h) -> p t h", h=2),
            in1=stabkb.unsqueeze(1).broadcast_to([128, NT, 2]), op=ALU.subtract),
            reads=[b_stabkb], writes=[b_biask])
        for g in range(4):
            pv4d = cx.psum[g].ap().rearrange("p (q h j) -> p q h j", h=2, j=64)
            p.do("vector", lambda v, g=g, pv4d=pv4d: v.tensor_tensor(
                out=pv4d, in0=pv4d,
                in1=th_bank_view(biask, g).unsqueeze(3).broadcast_to([128, 4, 2, 64]),
                op=ALU.add),
                reads=[cx.b_psum[g], b_biask], writes=[cx.b_psum[g]])
            p.do("scalar", lambda s, g=g, pv4d=pv4d: s.activation(
                out=stripe_bank_view(kp, g), in_=pv4d, func=AF.Exp),
                reads=[cx.b_psum[g]], writes=[b_kp])
        p.do("vector", lambda v: v.tensor_scalar(out=kp[:, :], in0=kp[:, :], scalar1=1e-4,
                                                 scalar2=None, op0=ALU.add),
             writes=[b_kp])

        pvkv = cx.psum[4].ap()[:, 0:128]
        pvks = cx.psum[5].ap()[:, 0:1]
        for tc in range(NT):
            p.do("tensor", lambda t, tc=tc, vt=vt: t.matmul(
                pvkv, kp[:, tc * 128:(tc + 1) * 128], vt[:, tc * 128:(tc + 1) * 128],
                start=(tc == 0), stop=(tc == NT - 1)),
                reads=[b_kp, b_vt], writes=[cx.b_psum[4]])
            p.do("tensor", lambda t, tc=tc: t.matmul(
                pvks, kp[:, tc * 128:(tc + 1) * 128], cx.ones_col,
                start=(tc == 0), stop=(tc == NT - 1)),
                reads=[b_kp, cx.b_ones_col], writes=[cx.b_psum[5]])
        p.do("gpsimd", lambda g: g.memset(kvblk, 0.0), writes=[b_kvblk])
        p.do("vector", lambda v: v.tensor_copy(out=kvblk[0:64, 0:64], in_=pvkv[0:64, 0:64]),
             reads=[cx.b_psum[4]], writes=[b_kvblk])
        p.do("vector", lambda v: v.tensor_copy(out=kvblk[64:128, 64:128],
                                               in_=pvkv[64:128, 64:128]),
             reads=[cx.b_psum[4]], writes=[b_kvblk])
        p.do("vector", lambda v: v.tensor_copy(out=kvblk[0:64, 128:129], in_=pvks[0:64, :]),
             reads=[cx.b_psum[5]], writes=[b_kvblk])
        p.do("vector", lambda v: v.tensor_copy(out=kvblk[64:128, 129:130], in_=pvks[64:128, :]),
             reads=[cx.b_psum[5]], writes=[b_kvblk])
        pvt = cx.psumT[0].ap()[:, 0:128]
        p.do("tensor", lambda t: t.transpose(pvt[0:2, :], kvblk[:, 128:130], cx.ident_bf),
             reads=[b_kvblk, cx.b_ident_bf], writes=[cx.b_psumT[0]])
        p.do("vector", lambda v: v.tensor_copy(out=ksr2[0:2, :], in_=pvt[0:2, :]),
             reads=[cx.b_psumT[0]], writes=[b_ksr2])
        p.do("tensor", lambda t: t.matmul(pv5[:, 128:256], cx.onesf2, ksr2[0:2, :],
                                          start=True, stop=True),
             reads=[cx.b_onesf, b_ksr2], writes=[cx.b_psum[5]])
        p.do("vector", lambda v: v.tensor_copy(out=ksrb, in_=pv5[:, 128:256]),
             reads=[cx.b_psum[5]], writes=[b_ksrb])

        if f < 7:
            l0_proj(f + 1, "qk")

        qe, b_qe = kp, b_kp
        for tc in range(NT):
            pb = tc % 4
            pv = cx.psum[pb].ap()[:, (tc // 4) * 128:(tc // 4) * 128 + 128]
            p.do("tensor", lambda t, tc=tc, pv=pv, f=f, qf=qf: t.matmul(
                pv, qf[:, tc * 128:(tc + 1) * 128], projbd[:, f * 128:(f + 1) * 128],
                start=(tc < 4), stop=(tc >= NT - 4)),
                reads=[b_qf, b_projbd], writes=[cx.b_psum[pb]])
        for g in range(4):
            pv4d = cx.psum[g].ap().rearrange("p (q h j) -> p q h j", h=2, j=64)
            p.do("vector", lambda v, g=g, pv4d=pv4d: v.reduce_max(
                out=th_bank_view(stq_all, g), in_=pv4d, axis=AX.X),
                reads=[cx.b_psum[g]], writes=[b_stq])
        p.do("gpsimd", lambda v: v.tensor_scalar(
            out=biasq, in0=dq_tm, scalar1=-1.0, scalar2=LNR,
            op0=ALU.mult, op1=ALU.add), reads=[b_dqt], writes=[b_biasq])
        p.do("gpsimd", lambda v: v.tensor_sub(out=biasq, in0=biasq, in1=stq_all),
             reads=[b_stq], writes=[b_biasq])
        for g in range(4):
            pv4d = cx.psum[g].ap().rearrange("p (q h j) -> p q h j", h=2, j=64)
            p.do("vector", lambda v, g=g, pv4d=pv4d: v.tensor_tensor(
                out=pv4d, in0=pv4d,
                in1=th_bank_view(biasq, g).unsqueeze(3).broadcast_to([128, 4, 2, 64]),
                op=ALU.add),
                reads=[cx.b_psum[g], b_biasq], writes=[cx.b_psum[g]])
            p.do("scalar", lambda s, g=g, pv4d=pv4d: s.activation(
                out=stripe_bank_view(qe, g), in_=pv4d, func=AF.Exp),
                reads=[cx.b_psum[g]], writes=[b_qe])
        if f < 7:
            l0_proj(f + 1, "v")
        qe4 = qe.rearrange("p (t h j) -> p t h j", h=2, j=64)
        qsc4 = qsc.rearrange("p (t h j) -> p t h j", h=2, j=64)
        nrm3 = nrm_all.rearrange("p (t h) -> p t h", h=2)
        for hf in range(2):
            qeh = qe4[:, hf * 8:(hf + 1) * 8]
            p.do("vector", lambda v, qeh=qeh: v.scalar_tensor_tensor(
                out=qsc4, in0=qeh, scalar=1e-4,
                in1=ksrb.rearrange("p (h j) -> p h j", h=2).unsqueeze(1).broadcast_to(
                    [128, 8, 2, 64]), op0=ALU.add, op1=ALU.mult),
                reads=[b_qe, b_ksrb], writes=[b_qsc])
            p.do("vector", lambda v, hf=hf: v.reduce_sum(
                out=nrm3[:, hf * 8:(hf + 1) * 8], in_=qsc4, axis=AX.X),
                reads=[b_qsc], writes=[b_nrm])
            p.do("vector", lambda v, hf=hf: v.tensor_scalar(
                out=nrm_all[:, hf * 16:(hf + 1) * 16], in0=nrm_all[:, hf * 16:(hf + 1) * 16],
                scalar1=0.01, scalar2=None, op0=ALU.max), writes=[b_nrm])
            p.do("vector", lambda v, hf=hf: v.reciprocal(
                out=nrm_all[:, hf * 16:(hf + 1) * 16],
                in_=nrm_all[:, hf * 16:(hf + 1) * 16]), writes=[b_nrm])
            p.do("vector", lambda v, qeh=qeh, hf=hf: v.scalar_tensor_tensor(
                out=qsc4, in0=qeh, scalar=1e-4,
                in1=nrm3[:, hf * 8:(hf + 1) * 8].unsqueeze(3).broadcast_to(
                    [128, 8, 2, 64]), op0=ALU.add, op1=ALU.mult),
                reads=[b_qe, b_nrm], writes=[b_qsc])
            transposeN_bf(cx, [qsc[:, i * 128:(i + 1) * 128] for i in range(8)],
                          [b_qsc] * 8, qp[:, hf * 1024:(hf + 1) * 1024], b_qp, pb=hf % 2)

        for tt in range(4):
            pb = tt % 2
            pv = cx.psum[pb].ap()[:, 0:512]
            p.do("tensor", lambda t, tt=tt, pv=pv: t.matmul(
                pv, kvblk[:, 0:128], qp[:, tt * 512:(tt + 1) * 512], start=True, stop=True),
                reads=[b_kvblk, b_qp], writes=[cx.b_psum[pb]])
            p.do("scalar", lambda s, tt=tt, pv=pv, f=f: s.activation(
                out=fma(att, f, tt * 512, 512), in_=pv, func=AF.Copy),
                reads=[cx.b_psum[pb]], writes=[b_att])

    for tc in range(NT):
        for ih in range(2):
            pb = (tc * 2 + ih) % 4
            pv = cx.psum[pb].ap()[:, 0:512]
            for kc in range(ND):
                p.do("tensor", lambda t, tc=tc, ih=ih, kc=kc, pv=pv: t.matmul(
                    pv, fma(att, kc, tc * 128, 128),
                    wo[:, kc * D + ih * 512: kc * D + ih * 512 + 512],
                    start=(kc == 0), stop=False),
                    reads=[b_att, b_wo], writes=[cx.b_psum[pb]])
            p.do("tensor", lambda t, ih=ih, pv=pv: t.matmul(
                pv, cx.ones_row[:, 0:128], bo_row[:, ih * 512:(ih + 1) * 512],
                start=False, stop=True),
                reads=[cx.b_ones_row, b_bor], writes=[cx.b_psum[pb]])
            p.do("vector", lambda v, tc=tc, ih=ih, pv=pv: v.tensor_add(
                out=cx.xt[tc][:, ih * 512:(ih + 1) * 512],
                in0=cx.xt[tc][:, ih * 512:(ih + 1) * 512], in1=pv),
                reads=[cx.b_psum[pb]], writes=[cx.bx[tc]])
            if ih == 1:
                ln_stats_tile(cx, tc)





def build_l0_ffn(cx, H):
    nc, p = cx.nc, cx.p
    p.fence()
    for a in (cx.BIG1, cx.BIG2, cx.STRIPE, cx.WT, cx.SCR):
        a.reset()

    ln2 = cx.BIG1.take(BF16, 128, ND * S); b_ln2 = Buf("ln2")
    h1 = cx.STRIPE.take(BF16, 128, 4 * S); b_h1 = Buf("h1")

    wg = [cx.WT.take(BF16, 128, 8 * 512) for _ in range(2)]
    wu = [cx.WT.take(BF16, 128, 8 * 512) for _ in range(2)]
    wd = cx.WT.take(BF16, 128, 4 * 1024)
    b_wg = [Buf("wg0"), Buf("wg1")]
    b_wu = [Buf("wu0"), Buf("wu1")]
    b_wd = Buf("wd")

    A = cx.SCR
    gwf = A.take(F32, 128, 64); b_gwf = Buf("gwf")
    gwhi = A.take(BF16, 128, 64); b_gwhi = Buf("gwhi")
    gwlo = A.take(BF16, 128, 64); b_gwlo = Buf("gwlo")
    lbr = A.take(F32, 1, 8); b_lbr = Buf("lbr")
    lbbc = A.take(F32, 128, 8); b_lbbc = Buf("lbbc")
    sco128 = A.take(F32, 128, 8 * NT); b_sco = Buf("sco128")
    t128 = A.take(F32, 128, 8 * NT); b_t128 = Buf("t128")
    m1a = A.take(F32, 128, NT); b_m1 = Buf("m1a")
    m2a = A.take(F32, 128, NT); b_m2 = Buf("m2a")
    den = A.take(F32, 128, NT); b_den = Buf("den")
    swall = A.take(BF16, 128, 16 * NT); b_swall = Buf("swall")
    swv = [swall.rearrange("p (t two e) -> p two t e", two=2, e=8)[:, i:i + 1].squeeze(1)
           for i in range(2)]
    sco_all = cx.SCR2.at(2560, F32, 128, 8 * NT) if globals().get("DBG") else None
    scb_all = cx.BIG2.take(F32, 128, 8 * NT) if globals().get("DBG") else None
    b_scoall, b_scball = Buf("sco_all"), Buf("scb_all")
    c_all = A.take(F32, 128, 8 * NT); b_c = Buf("c_all")
    red = A.take(F32, 16, 16); b_red = Buf("red")
    wm = A.take(F32, 1, 8); b_wm = Buf("wm")
    wmb = A.take(F32, 128, 8); b_wmb = Buf("wmb")
    sg = A.take(BF16, 128, 1024); b_sg = Buf("sg")
    lo_t = A.take(BF16, 128, 1024); b_lot = Buf("lo_t")
    lo_f = A.take(BF16, 128, 1024); b_lof = Buf("lo_f")
    ln32 = cx.SCR2.at(4096, F32, 128, 1024); b_ln32 = Buf("ln32")

    ccin = nc.dram_tensor(f"ccin{cx.ccn}", [1, 16], F32)
    ccout = nc.dram_tensor(f"ccout{cx.ccn}", [1, 16], F32)
    cx.ccn += 1

    p.do("sync", lambda s: s.dma_start(
        out=gwf.rearrange("p (c e) -> p c e", e=8),
        in_=H["l0f_gwT"].rearrange("(c p) e -> p c e", p=128)), writes=[b_gwf], dma=True)
    p.do("sync", lambda s: s.dma_start(out=lbr, in_=H["l0f_lb"]), writes=[b_lbr], dma=True)
    p.do("vector", lambda v: v.tensor_copy(out=gwhi, in_=gwf), reads=[b_gwf], writes=[b_gwhi])
    p.do("vector", lambda v: v.tensor_sub(out=gwlo, in0=gwf, in1=gwhi),
         reads=[b_gwf, b_gwhi], writes=[b_gwlo])
    pv5 = cx.psum[5].ap()
    p.do("tensor", lambda t: t.matmul(pv5[:, 0:8], cx.onesf_row, lbr, start=True, stop=True),
         reads=[cx.b_onesf, b_lbr], writes=[cx.b_psum[5]])
    p.do("vector", lambda v: v.tensor_copy(out=lbbc, in_=pv5[:, 0:8]),
         reads=[cx.b_psum[5]], writes=[b_lbbc])

    pvG = cx.psum[4].ap()
    pvC = cx.psum[5].ap()

    stats = [None, None, None]

    def gate_cb(tc, lnd, b_lnd):
        _m, _r, _bst = stats
        p.do("vector", lambda v, tc=tc, _m=_m, _r=_r: v.tensor_scalar(
            out=ln32, in0=cx.xt[tc], scalar1=_m[:, tc:tc + 1], scalar2=_r[:, tc:tc + 1],
            op0=ALU.subtract, op1=ALU.mult), reads=[cx.bx[tc], _bst], writes=[b_ln32])
        p.do("gpsimd", lambda v: v.tensor_sub(out=lo_t, in0=ln32, in1=lnd),
             reads=[b_ln32, b_lnd], writes=[b_lot])
        transposeN_bf(cx, [lo_t[:, c * 128:(c + 1) * 128] for c in range(ND)],
                      [b_lot] * ND, lo_f, b_lof, pb=(tc + 1) % 2)
        for kc in range(ND):
            p.do("tensor", lambda t, kc=kc, tc=tc: t.matmul(
                pvG[:, tc * 8:tc * 8 + 8], fma(ln2, kc, tc * 128, 128),
                gwhi[:, kc * 8:(kc + 1) * 8],
                start=(kc == 0), stop=False),
                reads=[b_ln2, b_gwhi], writes=[cx.b_psum[4]])
        for kc in range(ND):
            p.do("tensor", lambda t, kc=kc, tc=tc: t.matmul(
                pvG[:, tc * 8:tc * 8 + 8], fma(ln2, kc, tc * 128, 128),
                gwlo[:, kc * 8:(kc + 1) * 8],
                start=False, stop=False),
                reads=[b_ln2, b_gwlo], writes=[cx.b_psum[4]])
        for kc in range(ND):
            p.do("tensor", lambda t, kc=kc, tc=tc: t.matmul(
                pvG[:, tc * 8:tc * 8 + 8], lo_f[:, kc * 128:(kc + 1) * 128],
                gwhi[:, kc * 8:(kc + 1) * 8],
                start=False, stop=(kc == ND - 1)),
                reads=[b_lof, b_gwhi], writes=[cx.b_psum[4]])

    layernorm(cx, ln2, b_ln2, per_tile_cb=gate_cb, stats_out=stats,
              pre_stats=True)

    s3 = sco128.rearrange("p (t e) -> p t e", e=8)
    t3 = t128.rearrange("p (t e) -> p t e", e=8)
    p.do("vector", lambda v: v.tensor_tensor(
        out=s3, in0=pvG[:, 0:8 * NT].rearrange("p (t e) -> p t e", e=8),
        in1=lbbc.unsqueeze(1).broadcast_to([128, NT, 8]), op=ALU.add),
        reads=[cx.b_psum[4], b_lbbc], writes=[b_sco])
    p.do("scalar", lambda s: s.activation(out=sco128, in_=sco128, func=AF.Sigmoid),
         reads=[], writes=[b_sco])
    p.do("vector", lambda v: v.reduce_max(out=m1a, in_=s3, axis=AX.X),
         reads=[b_sco], writes=[b_m1])
    p.do("vector", lambda v: v.tensor_tensor(
        out=t3, in0=s3, in1=m1a.unsqueeze(2).broadcast_to([128, NT, 8]), op=ALU.is_ge),
        reads=[b_sco, b_m1], writes=[b_t128])
    p.do("vector", lambda v: v.scalar_tensor_tensor(
        out=t128, in0=t128, scalar=-1e9, in1=sco128, op0=ALU.mult, op1=ALU.add),
        reads=[b_sco], writes=[b_t128])
    p.do("vector", lambda v: v.reduce_max(out=m2a, in_=t3, axis=AX.X),
         reads=[b_t128], writes=[b_m2])
    p.do("vector", lambda v: v.tensor_tensor(
        out=swv[0], in0=s3, in1=m2a.unsqueeze(2).broadcast_to([128, NT, 8]), op=ALU.is_ge),
        reads=[b_sco, b_m2], writes=[b_swall])
    p.do("vector", lambda v: v.tensor_add(out=den, in0=m1a, in1=m2a),
         reads=[b_m1, b_m2], writes=[b_den])
    p.do("vector", lambda v: v.tensor_scalar(out=den, in0=den, scalar1=1e-8, scalar2=None,
                                             op0=ALU.add), writes=[b_den])
    p.do("vector", lambda v: v.reciprocal(out=den, in_=den), writes=[b_den])
    p.do("vector", lambda v: v.tensor_tensor(
        out=t3, in0=s3, in1=den.unsqueeze(2).broadcast_to([128, NT, 8]), op=ALU.mult),
        reads=[b_sco, b_den], writes=[b_t128])
    p.do("vector", lambda v: v.tensor_tensor(
        out=swv[1], in0=t3, in1=swv[0], op=ALU.mult),
        reads=[b_t128], writes=[b_swall])
    for tc in range(NT):
        p.do("tensor", lambda t, tc=tc: t.matmul(
            pvC[0:16, 8:9], swall[:, tc * 16:(tc + 1) * 16], cx.ones_col,
            start=(tc == 0), stop=(tc == NT - 1)),
            reads=[b_swall, cx.b_ones_col], writes=[cx.b_psum[5]])

    p.do("vector", lambda v: v.tensor_copy(out=red[0:16, 0:1], in_=pvC[0:16, 8:9]),
         reads=[cx.b_psum[5]], writes=[b_red])
    p.do("tensor", lambda t: t.transpose(pvC[0:1, 10:26], red[0:16, 0:1],
                                         cx.ident_f[0:16, 0:16]),
         reads=[b_red, cx.b_ident_f], writes=[cx.b_psum[5]])
    p.do("vector", lambda v: v.tensor_copy(out=red[0:1, 0:16], in_=pvC[0:1, 10:26]),
         reads=[cx.b_psum[5]], writes=[b_red])
    dma1 = p.do("gpsimd", lambda g: g.dma_start(out=ccin.ap(), in_=red[0:1, 0:16]),
                reads=[b_red], dma=True)
    cc = p.do("gpsimd", lambda g: g.collective_compute(
        "AllReduce", ALU.add, replica_groups=[list(range(8))],
        ins=[ccin.ap()], outs=[ccout.ap()]), deps=[dma1])
    wmd = nc.dram_tensor(f"wmd{cx.ccn}", [1, 8], F32)

    def emit_wmean():
        p.do("gpsimd", lambda g: g.dma_start(out=red[0:1, 0:16], in_=ccout.ap()),
             deps=[cc], writes=[b_red], dma=True)
        p.do("vector", lambda v: v.tensor_scalar(out=wm, in0=red[0:1, 0:8], scalar1=1.0,
                                                 scalar2=None, op0=ALU.max),
             reads=[b_red], writes=[b_wm])
        p.do("vector", lambda v: v.reciprocal(out=wm, in_=wm), writes=[b_wm])
        p.do("vector", lambda v: v.tensor_mul(out=wm, in0=wm, in1=red[0:1, 8:16]),
             reads=[b_red], writes=[b_wm])
        d_out = p.do("sync", lambda s: s.dma_start(out=wmd.ap(), in_=wm),
                     reads=[b_wm], dma=True)
        p.do("sync", lambda s: s.dma_start(
            out=wmb, in_=wmd.ap().broadcast_to([128, 8])), writes=[b_wmb],
            deps=[d_out], dma=True)
        p.do("vector", lambda v: v.tensor_tensor(
            out=c_all.rearrange("p (t e) -> p t e", e=8),
            in0=swv[0],
            in1=wmb.unsqueeze(1).broadcast_to([128, NT, 8]), op=ALU.mult),
            reads=[b_swall, b_wmb], writes=[b_c])

    if globals().get("DBG"):
        dsel = nc.dram_tensor("d_sel", [128, 8 * NT], F32, kind="ExternalOutput").ap()
        dcal = nc.dram_tensor("d_c", [128, 8 * NT], F32, kind="ExternalOutput").ap()
        dred = nc.dram_tensor("d_red", [1, 16], F32, kind="ExternalOutput").ap()
        dwm = nc.dram_tensor("d_wm", [1, 8], F32, kind="ExternalOutput").ap()
        p.do("sync", lambda s: s.dma_start(out=dsel, in_=sel_all), reads=[b_sel], dma=True)
        p.do("sync", lambda s: s.dma_start(out=dcal, in_=c_all), reads=[b_c], dma=True)
        p.do("sync", lambda s: s.dma_start(out=dred, in_=red[0:1, 0:16]), reads=[b_red], dma=True)
        p.do("sync", lambda s: s.dma_start(out=dwm, in_=wm), reads=[b_wm], dma=True)
        dsco = nc.dram_tensor("d_sco", [128, 8 * NT], F32, kind="ExternalOutput").ap()
        p.do("sync", lambda s: s.dma_start(out=dsco, in_=sco_all), reads=[b_scoall], dma=True)
        dscb = nc.dram_tensor("d_scb", [128, 8 * NT], F32, kind="ExternalOutput").ap()
        p.do("sync", lambda s: s.dma_start(out=dscb, in_=scb_all), reads=[b_scball], dma=True)
        dln2 = nc.dram_tensor("d_ln2", [128, ND * S], BF16, kind="ExternalOutput").ap()
        p.do("sync", lambda s: s.dma_start(out=dln2, in_=ln2), reads=[b_ln2], dma=True)
        dgwhi = nc.dram_tensor("d_gwhi", [128, 64], BF16, kind="ExternalOutput").ap()
        p.do("sync", lambda s: s.dma_start(out=dgwhi, in_=gwhi), reads=[b_gwhi], dma=True)
        dgwlo = nc.dram_tensor("d_gwlo", [128, 64], BF16, kind="ExternalOutput").ap()
        p.do("sync", lambda s: s.dma_start(out=dgwlo, in_=gwlo), reads=[b_gwlo], dma=True)
        dgwf = nc.dram_tensor("d_gwf", [128, 64], F32, kind="ExternalOutput").ap()
        p.do("sync", lambda s: s.dma_start(out=dgwf, in_=gwf), reads=[b_gwf], dma=True)

    def load_expert(kind, idx, slot):
        if kind == "sh":
            srcg = H["l0f_sgwT"][:, idx * 512:(idx + 1) * 512]
            srcu = H["l0f_suwT"][:, idx * 512:(idx + 1) * 512]
        else:
            srcg = H["l0f_egwT"][idx]
            srcu = H["l0f_euwT"][idx]
        p.do("gpsimd", lambda g: g.dma_start(
            out=wg[slot].rearrange("p (c i) -> p c i", i=512),
            in_=srcg.rearrange("(c p) i -> p c i", p=128)), writes=[b_wg[slot]], dma=True)
        p.do("gpsimd", lambda g: g.dma_start(
            out=wu[slot].rearrange("p (c i) -> p c i", i=512),
            in_=srcu.rearrange("(c p) i -> p c i", p=128)), writes=[b_wu[slot]], dma=True)

    def load_expert_dw(kind, idx):
        if kind == "sh":
            srcd = H["l0f_sdwT"][idx * 512:(idx + 1) * 512, :]
        else:
            srcd = H["l0f_edwT"][idx]
        p.do("gpsimd", lambda g: g.dma_start(
            out=wd.rearrange("p (c i) -> p c i", i=1024),
            in_=srcd.rearrange("(c p) i -> p c i", p=128)), writes=[b_wd], dma=True)

    experts = [("sh", 0), ("sh", 1)] + [("rt", e) for e in range(8)]
    load_expert(*experts[0], 0)
    for ei, (kind, idx) in enumerate(experts):
        if ei == 1:
            emit_wmean()
        slot = ei % 2
        if ei + 1 < len(experts):
            load_expert(*experts[ei + 1], 1 - slot)
        load_expert_dw(kind, idx)
        for ic in range(4):
            for tt in range(4):
                pbg = (tt % 2)
                pbu = 2 + (tt % 2)
                pg = cx.psum[pbg].ap()[:, 0:512]
                pu = cx.psum[pbu].ap()[:, 0:512]
                for kc in range(ND):
                    p.do("tensor", lambda t, ic=ic, tt=tt, kc=kc, pg=pg, slot=slot: t.matmul(
                        pg, wg[slot][:, kc * 512 + ic * 128: kc * 512 + ic * 128 + 128],
                        fma(ln2, kc, tt * 512, 512),
                        start=(kc == 0), stop=(kc == ND - 1)),
                        reads=[b_wg[slot], b_ln2], writes=[cx.b_psum[pbg]])
                    p.do("tensor", lambda t, ic=ic, tt=tt, kc=kc, pu=pu, slot=slot: t.matmul(
                        pu, wu[slot][:, kc * 512 + ic * 128: kc * 512 + ic * 128 + 128],
                        fma(ln2, kc, tt * 512, 512),
                        start=(kc == 0), stop=(kc == ND - 1)),
                        reads=[b_wu[slot], b_ln2], writes=[cx.b_psum[pbu]])
                sgh = sg[:, (tt % 2) * 512:(tt % 2) * 512 + 512]
                p.do("scalar", lambda s, sgh=sgh, pg=pg: s.activation(
                    out=sgh, in_=pg, func=AF.Silu),
                    reads=[cx.b_psum[pbg]], writes=[b_sg])
                p.do("vector", lambda v, ic=ic, tt=tt, sgh=sgh, pu=pu: v.tensor_tensor(
                    out=h1[:, ic * S + tt * 512: ic * S + tt * 512 + 512],
                    in0=pu, in1=sgh, op=ALU.mult),
                    reads=[cx.b_psum[pbu], b_sg], writes=[b_h1])
        for tc in range(NT):
            for dc in range(2):
                pbd = 4 + (tc * 2 + dc) % 2
                pd = cx.psum[pbd].ap()[:, 0:512]
                for ic in range(4):
                    p.do("tensor", lambda t, tc=tc, dc=dc, ic=ic, pd=pd: t.matmul(
                        pd, h1[:, ic * S + tc * 128: ic * S + tc * 128 + 128],
                        wd[:, ic * 1024 + dc * 512: ic * 1024 + dc * 512 + 512],
                        start=(ic == 0), stop=(ic == 3)),
                        reads=[b_h1, b_wd], writes=[cx.b_psum[pbd]])
                if kind == "sh":
                    p.do("vector", lambda v, tc=tc, dc=dc, pd=pd: v.tensor_add(
                        out=cx.xt[tc][:, dc * 512:(dc + 1) * 512],
                        in0=cx.xt[tc][:, dc * 512:(dc + 1) * 512], in1=pd),
                        reads=[cx.b_psum[pbd]], writes=[cx.bx[tc]])
                else:
                    p.do("vector", lambda v, tc=tc, dc=dc, pd=pd, idx=idx: v.scalar_tensor_tensor(
                        out=cx.xt[tc][:, dc * 512:(dc + 1) * 512],
                        in0=pd, scalar=c_all[:, tc * 8 + idx: tc * 8 + idx + 1],
                        in1=cx.xt[tc][:, dc * 512:(dc + 1) * 512],
                        op0=ALU.mult, op1=ALU.add),
                        reads=[cx.b_psum[pbd], b_c], writes=[cx.bx[tc]])
                if dc == 1 and ei == len(experts) - 1:
                    ln_stats_tile(cx, tc)





def build_l1_attn(cx, H):
    nc, p = cx.nc, cx.p
    p.fence()
    for a in (cx.BIG1, cx.BIG2, cx.STRIPE, cx.WT, cx.SCR):
        a.reset()

    ln3 = cx.BIG1.take(BF16, 128, ND * S); b_ln3 = Buf("ln3")
    rem = cx.BIG2.take(BF16, 128, ND * S); b_rem = Buf("rem")

    A = cx.SCR
    apool = A.take(BF16, 128, 4); b_apool = Buf("apool")
    pooled = A.take(BF16, 128, 512); b_pooled = Buf("pooled")
    rkvf = A.take(BF16, 128, 256); b_rkvf = Buf("rkvf")
    rvT = A.take(BF16, 128, 128); b_rvT = Buf("rvT")
    qkvb = A.take(F32, 128, 24); b_qkvb = Buf("qkvb")
    qb_s = A.take(F32, 128, 8); b_qbs = Buf("qb_s")
    rqb = A.take(F32, 128, 2); b_rqb = Buf("rqb")
    rqb_s = A.take(F32, 128, 2); b_rqbs = Buf("rqb_s")
    rkvb = A.take(F32, 128, 4); b_rkvb = Buf("rkvb")
    routb = A.take(F32, 128, 8); b_routb = Buf("routb")
    outb_row = A.take(BF16, 1, D); b_outbr = Buf("outb_row")
    glt = A.take(BF16, 128, 128); b_glt = Buf("glt")
    grt = A.take(BF16, 128, 128); b_grt = Buf("grt")
    gg = A.take(F32, 128, 32); b_gg = Buf("gg")
    g1 = A.take(F32, 128, 32); b_g1 = Buf("g1")
    rs22 = [A.take(F32, 128, 2) for _ in range(2)]
    b_rs22 = [Buf("rs20"), Buf("rs21")]
    letile2 = [A.take(BF16, 128, 128) for _ in range(2)]
    b_let2 = [Buf("let0"), Buf("let1")]
    locmask = A.take(BF16, 128, 256); b_locm = Buf("locmask")
    latmask = A.take(BF16, 128, NT * 64); b_latm = Buf("latmask")

    p.do("sync", lambda s: s.dma_start(out=qkvb, in_=H["l1_qkvb_c"]), writes=[b_qkvb], dma=True)
    p.do("sync", lambda s: s.dma_start(out=rqb, in_=H["l1_rqb_c"]), writes=[b_rqb], dma=True)
    p.do("sync", lambda s: s.dma_start(out=rkvb, in_=H["l1_rkvb_c"]), writes=[b_rkvb], dma=True)
    p.do("sync", lambda s: s.dma_start(out=routb, in_=H["l1_routb_c"]), writes=[b_routb], dma=True)
    p.do("gpsimd", lambda g: g.dma_start(out=outb_row, in_=H["l1_outb_r"]), writes=[b_outbr], dma=True)
    p.do("vector", lambda v: v.tensor_scalar(out=qb_s, in0=qkvb[:, 0:8], scalar1=0.125,
                                             scalar2=None, op0=ALU.mult),
         reads=[b_qkvb], writes=[b_qbs])
    p.do("vector", lambda v: v.tensor_scalar(out=rqb_s, in0=rqb, scalar1=0.125,
                                             scalar2=None, op0=ALU.mult),
         reads=[b_rqb], writes=[b_rqbs])
    glr = A.take(BF16, 1, 128); b_glr = Buf("glr")
    grr = A.take(BF16, 1, 128); b_grr = Buf("grr")
    p.do("gpsimd", lambda g: g.dma_start(out=glr, in_=H["l1_gl_r"]), writes=[b_glr], dma=True)
    p.do("gpsimd", lambda g: g.dma_start(out=grr, in_=H["l1_gr_r"]), writes=[b_grr], dma=True)
    pv5 = cx.psum[5].ap()
    p.do("tensor", lambda t: t.matmul(pv5[:, 0:128], cx.ones_row[:, 0:128], glr,
                                      start=True, stop=False),
         reads=[cx.b_ones_row, b_glr], writes=[cx.b_psum[5]])
    p.do("tensor", lambda t: t.matmul(pv5[:, 128:256], cx.ones_row[:, 0:128], grr,
                                      start=False, stop=True),
         reads=[cx.b_ones_row, b_grr], writes=[cx.b_psum[5]])
    p.do("vector", lambda v: v.tensor_copy(out=glt, in_=pv5.bitcast(F32)[:, 0:128] if False else pv5[:, 0:128]),
         reads=[cx.b_psum[5]], writes=[b_glt])
    p.do("vector", lambda v: v.tensor_copy(out=grt, in_=pv5[:, 128:256]),
         reads=[cx.b_psum[5]], writes=[b_grt])
    p.do("gpsimd", lambda g: g.memset(locmask, 1.0), writes=[b_locm])
    p.do("gpsimd", lambda g: g.affine_select(locmask[:, 0:128], locmask[:, 0:128],
                                             [[-1, 128]], ALU.is_ge, 0.0,
                                             base=0, channel_multiplier=1), writes=[b_locm])
    p.do("gpsimd", lambda g: g.affine_select(locmask[:, 128:256], locmask[:, 128:256],
                                             [[1, 128]], ALU.is_ge, 0.0,
                                             base=-1, channel_multiplier=-1), writes=[b_locm])
    p.do("gpsimd", lambda g: g.memset(latmask, 1.0), writes=[b_latm])
    for tc in range(NT):
        p.do("gpsimd", lambda g, tc=tc: g.affine_select(
            latmask[:, tc * 64:(tc + 1) * 64], latmask[:, tc * 64:(tc + 1) * 64],
            [[-32, 64]], ALU.is_ge, 0.0, base=128 * tc - 32, channel_multiplier=1),
            writes=[b_latm])
    p.do("gpsimd", lambda g: g.memset(apool, 1.0 / 32.0), writes=[b_apool])
    p.do("gpsimd", lambda g: g.affine_select(apool, apool, [[-32, 4]], ALU.is_ge, 0.0,
                                             base=0, channel_multiplier=1), writes=[b_apool])
    p.do("gpsimd", lambda g: g.affine_select(apool, apool, [[32, 4]], ALU.is_ge, 0.0,
                                             base=31, channel_multiplier=-1), writes=[b_apool])

    wrq = cx.WT.take(BF16, 128, ND * 256); b_wrq = Buf("wrq")
    wrkv = cx.WT.take(BF16, 128, ND * 512); b_wrkv = Buf("wrkv")
    wrout = cx.WT.take(BF16, 128, 2 * D); b_wrout = Buf("wrout")
    p.do("gpsimd", lambda g: g.dma_start(
        out=wrq.rearrange("p (c i) -> p c i", i=256),
        in_=H["l1_rqw"].rearrange("(c p) i -> p c i", p=128)), writes=[b_wrq], dma=True)
    p.do("gpsimd", lambda g: g.dma_start(
        out=wrkv.rearrange("p (c i) -> p c i", i=512),
        in_=H["l1_rkvw"].rearrange("(c p) i -> p c i", p=128)), writes=[b_wrkv], dma=True)
    p.do("gpsimd", lambda g: g.dma_start(
        out=wrout.rearrange("p (c i) -> p c i", i=D),
        in_=H["l1_routw"].rearrange("(c p) i -> p c i", p=128)), writes=[b_wrout], dma=True)

    pv4 = cx.psum[4].ap()

    def pool_cb(tc, lnd, b_lnd):
        for dc in range(ND):
            p.do("tensor", lambda t, tc=tc, dc=dc: t.matmul(
                pv4[:, dc * 64 + 4 * tc: dc * 64 + 4 * tc + 4],
                lnd[:, dc * 128:(dc + 1) * 128], apool,
                start=(tc == 0 and dc == 0), stop=(tc == NT - 1 and dc == ND - 1)),
                reads=[b_lnd, b_apool], writes=[cx.b_psum[4]])

    layernorm(cx, ln3, b_ln3, per_tile_cb=pool_cb, pre_stats=True)
    p.do("vector", lambda v: v.tensor_copy(out=pooled, in_=pv4[:, 0:512]),
         reads=[cx.b_psum[4]], writes=[b_pooled])
    if globals().get("L1STAGE", 99) < 2:
        return

    rqf = cx.STRIPE.take(BF16, 128, 2 * S); b_rqf = Buf("rqf")
    latn = cx.STRIPE.take(BF16, 128, 2 * S); b_latn = Buf("latn")
    for oc in range(2):
        for tt in range(4):
            pb = tt % 2
            pv = cx.psum[pb].ap()[:, 0:512]
            for kc in range(ND):
                p.do("tensor", lambda t, oc=oc, tt=tt, kc=kc, pv=pv: t.matmul(
                    pv, wrq[:, kc * 256 + oc * 128: kc * 256 + oc * 128 + 128],
                    fma(ln3, kc, tt * 512, 512),
                    start=(kc == 0), stop=(kc == ND - 1)),
                    reads=[b_wrq, b_ln3], writes=[cx.b_psum[pb]])
            p.do("scalar", lambda s, oc=oc, tt=tt, pv=pv: s.activation(
                out=rqf[:, oc * S + tt * 512: oc * S + tt * 512 + 512], in_=pv,
                func=AF.Identity, bias=rqb_s[:, oc:oc + 1], scale=0.125),
                reads=[cx.b_psum[pb], b_rqbs], writes=[b_rqf])

    if globals().get("L1STAGE", 99) < 3:
        return
    for oc in range(4):
        pv = cx.psum[5].ap()[:, oc * 64:(oc + 1) * 64]
        for kc in range(ND):
            p.do("tensor", lambda t, oc=oc, kc=kc, pv=pv: t.matmul(
                pv, wrkv[:, kc * 512 + oc * 128: kc * 512 + oc * 128 + 128],
                pooled[:, kc * 64:(kc + 1) * 64],
                start=(kc == 0), stop=(kc == ND - 1)),
                reads=[b_wrkv, b_pooled], writes=[cx.b_psum[5]])
        p.do("scalar", lambda s, oc=oc, pv=pv: s.activation(
            out=rkvf[:, oc * 64:(oc + 1) * 64], in_=pv, func=AF.Identity,
            bias=rkvb[:, oc:oc + 1]),
            reads=[cx.b_psum[5], b_rkvb], writes=[b_rkvf])

    letP2 = [cx.SCR2.at(4096, BF16, 128, 512), cx.SCR2.at(5120, BF16, 128, 512)]
    b_letP2 = [Buf("letP0"), Buf("letP1")]
    latsum2 = [A.take(F32, 128, 8) for _ in range(2)]
    b_latsum2 = [Buf("lats0"), Buf("lats1")]
    for tp in range(NT // 2):
        letP, b_letP = letP2[tp % 2], b_letP2[tp % 2]
        lsum, b_lsum = latsum2[tp % 2], b_latsum2[tp % 2]
        boff = 256 * (tp % 2)
        for t2 in range(2):
            tc = 2 * tp + t2
            for lh in range(4):
                hh, ch = lh % 2, lh // 2
                pv = cx.psum[hh].ap()
                p.do("tensor", lambda t, tc=tc, t2=t2, pv=pv, hh=hh, ch=ch,
                     boff=boff: t.matmul(
                        pv[:, boff + t2 * 128 + ch * 64: boff + t2 * 128 + ch * 64 + 64],
                        rqf[64 * hh: 64 * hh + 64, ch * S + tc * 128: ch * S + tc * 128 + 128],
                        rkvf[64 * hh: 64 * hh + 64, ch * 64: ch * 64 + 64],
                        start=True, stop=True),
                    reads=[b_rqf, b_rkvf], writes=[cx.b_psum[hh]])
        for hh in range(2):
            pv = cx.psum[hh].ap()
            p.do("scalar", lambda s, pv=pv, hh=hh, letP=letP, boff=boff: s.activation(
                out=letP.rearrange("p (t2 ch hh j) -> p hh t2 ch j", t2=2, ch=2,
                                   j=64)[:, hh:hh + 1].squeeze(1),
                in_=pv[:, boff:boff + 256].rearrange("p (t2 ch j) -> p t2 ch j",
                                                     t2=2, ch=2),
                func=AF.Exp),
                reads=[cx.b_psum[hh]], writes=[b_letP])
        le4 = letP.rearrange("p (t2 lh j) -> p t2 lh j", t2=2, lh=4)
        p.do("vector", lambda v, tp=tp, le4=le4: v.tensor_tensor(
            out=le4, in0=le4,
            in1=latmask.rearrange("p (t j) -> p t j", j=64)[:, 2 * tp:2 * tp + 2]
            .unsqueeze(2).broadcast_to([128, 2, 4, 64]), op=ALU.mult),
            reads=[b_latm], writes=[b_letP])
        p.do("vector", lambda v, le4=le4, lsum=lsum: v.reduce_sum(
            out=lsum.rearrange("p (t2 lh) -> p t2 lh", t2=2), in_=le4, axis=AX.X),
            reads=[b_letP], writes=[b_lsum])
        p.do("vector", lambda v, lsum=lsum: v.tensor_scalar(
            out=lsum, in0=lsum, scalar1=1e-30, scalar2=None, op0=ALU.add),
            writes=[b_lsum])
        p.do("vector", lambda v, lsum=lsum: v.reciprocal(out=lsum, in_=lsum),
             writes=[b_lsum])
        p.do("vector", lambda v, le4=le4, lsum=lsum: v.tensor_tensor(
            out=le4, in0=le4,
            in1=lsum.rearrange("p (t2 lh) -> p t2 lh", t2=2).unsqueeze(3)
            .broadcast_to([128, 2, 4, 64]), op=ALU.mult),
            reads=[b_lsum], writes=[b_letP])
        for ch in range(2):
            transposeN_bf(cx, [letP[:, t2 * 256 + ch * 128: t2 * 256 + ch * 128 + 128]
                               for t2 in range(2)], [b_letP] * 2,
                          latn[:, ch * S + 2 * tp * 128: ch * S + 2 * tp * 128 + 256],
                          b_latn, pb=tp % 2)

    if globals().get("L1STAGE", 99) < 4:
        return
    for i in range(2):
        pvt = cx.psumT[i % 2].ap()[0:64, 0:128]
        p.do("tensor", lambda t, i=i, pvt=pvt: t.transpose(
            pvt, rkvf[:, (2 + i) * 64: (3 + i) * 64], cx.ident_bf),
            reads=[b_rkvf, cx.b_ident_bf], writes=[cx.b_psumT[i % 2]])
        p.do("vector", lambda v, i=i, pvt=pvt: v.tensor_copy(
            out=rvT[0:64, i * 64: i * 64 + 64], in_=pvt[0:64, 0:64]),
            reads=[cx.b_psumT[i % 2]], writes=[b_rvT])
        p.do("vector", lambda v, i=i, pvt=pvt: v.tensor_copy(
            out=rvT[64:128, i * 64: i * 64 + 64], in_=pvt[0:64, 64:128]),
            reads=[cx.b_psumT[i % 2]], writes=[b_rvT])

    latf = rqf
    for lh in range(4):
        hh, ch = lh % 2, lh // 2
        for tt in range(4):
            pb = 2 + tt % 2
            pv = cx.psum[pb].ap()[0:64, 0:512]
            p.do("tensor", lambda t, lh=lh, tt=tt, pv=pv, hh=hh, ch=ch: t.matmul(
                pv, rvT[64 * hh: 64 * hh + 64, ch * 64: ch * 64 + 64],
                latn[64 * hh: 64 * hh + 64, ch * S + tt * 512: ch * S + tt * 512 + 512],
                start=True, stop=True),
                reads=[b_rvT, b_latn], writes=[cx.b_psum[pb]])
            p.do("scalar", lambda s, lh=lh, tt=tt, pv=pv, hh=hh, ch=ch: s.activation(
                out=latf[64 * hh: 64 * hh + 64, ch * S + tt * 512: ch * S + tt * 512 + 512],
                in_=pv, func=AF.Copy),
                reads=[cx.b_psum[pb]], writes=[b_rqf])

    for oc in range(ND):
        for tt in range(4):
            pb = tt % 2
            pv = cx.psum[pb].ap()[:, 0:512]
            for kc in range(2):
                p.do("tensor", lambda t, oc=oc, tt=tt, kc=kc, pv=pv: t.matmul(
                    pv, wrout[:, kc * D + oc * 128: kc * D + oc * 128 + 128],
                    latf[:, kc * S + tt * 512: kc * S + tt * 512 + 512],
                    start=(kc == 0), stop=(kc == 1)),
                    reads=[b_wrout, b_rqf], writes=[cx.b_psum[pb]])
            p.do("scalar", lambda s, oc=oc, tt=tt, pv=pv: s.activation(
                out=fma(rem, oc, tt * 512, 512), in_=pv, func=AF.Identity,
                bias=routb[:, oc:oc + 1]),
                reads=[cx.b_psum[pb], b_routb], writes=[b_rem])

    if globals().get("DBG"):
        for nm, ap_, b_ in (("d_pooled", pooled, b_pooled), ("d_rkvf", rkvf, b_rkvf),
                            ("d_latn", latn, b_latn), ("d_latf", latf, b_rqf),
                            ("d_rvT", rvT, b_rvT)):
            dt_ = nc.dram_tensor(nm, [ap_.shape[0], ap_.shape[1]], BF16,
                                 kind="ExternalOutput").ap()
            p.do("sync", lambda s, dt_=dt_, ap_=ap_: s.dma_start(out=dt_, in_=ap_),
                 reads=[b_], dma=True)
        drem = nc.dram_tensor("d_rem", [128, ND * S], BF16, kind="ExternalOutput").ap()
        p.do("sync", lambda s: s.dma_start(out=drem, in_=rem), reads=[b_rem], dma=True)

    if globals().get("L1STAGE", 99) < 5:
        return
    p.fence()
    cx.STRIPE.reset()
    cx.WT.reset()
    wout = cx.WT.take(BF16, 128, ND * D); b_wout = Buf("wout")
    p.do("gpsimd", lambda g: g.dma_start(
        out=wout.rearrange("p (c i) -> p c i", i=D),
        in_=H["l1_outw"].rearrange("(c p) i -> p c i", p=128)), writes=[b_wout], dma=True)
    qf2 = [cx.STRIPE.take(BF16, 128, S), cx.WT.take(BF16, 128, S)]
    b_qf2 = [Buf("l1qf0"), Buf("l1qf1")]
    kf2 = [cx.STRIPE.take(BF16, 128, S), cx.WT.take(BF16, 128, S)]
    b_kf2 = [Buf("l1kf0"), Buf("l1kf1")]
    vt2 = [cx.STRIPE.take(BF16, 128, S), cx.WT.take(BF16, 128, S)]
    b_vt2 = [Buf("l1vt0"), Buf("l1vt1")]
    locf = cx.STRIPE.take(BF16, 128, S); b_loc = Buf("locf")
    remt = cx.STRIPE.take(BF16, 128, S); b_remt = Buf("remt")
    expst4 = [cx.SCR2.at(4096 + i * 1024, BF16, 128, 512) for i in range(4)]
    b_expst4 = [Buf(f"expst{i}") for i in range(4)]
    vstage = cx.SCR2.at(4096, BF16, 128, 2048)
    rs8g2 = [A.take(F32, 128, 8) for _ in range(2)]
    b_rs8g2 = [Buf("rs8g0"), Buf("rs8g1")]

    wsl = [cx.WT.take(BF16, 128, ND * 128) for _ in range(6)]
    b_wsl = [Buf(f"l1w{i}") for i in range(6)]

    def load_pair_w(f, slot):
        for j, coloff in enumerate((0, D, 2 * D)):
            src = H["l1_qkvw"][:, coloff + f * 128: coloff + (f + 1) * 128]
            src = src.rearrange("(c p) i -> p c i", p=128)
            t_, b_ = wsl[slot * 3 + j], b_wsl[slot * 3 + j]
            p.do("gpsimd", lambda g, t_=t_, src=src: g.dma_start(
                out=t_.rearrange("p (c i) -> p c i", i=128), in_=src),
                writes=[b_], dma=True)

    def l1_proj(f, which):
        slot = f % 2
        wq_, wk_, wv_ = wsl[slot * 3], wsl[slot * 3 + 1], wsl[slot * 3 + 2]
        b_wq_, b_wk_, b_wv_ = b_wsl[slot * 3], b_wsl[slot * 3 + 1], b_wsl[slot * 3 + 2]
        qf, b_qf = qf2[f % 2], b_qf2[f % 2]
        kf, b_kf = kf2[f % 2], b_kf2[f % 2]
        vt, b_vt = vt2[f % 2], b_vt2[f % 2]
        specs = {"qk": ((wq_, b_wq_, qf, b_qf, qb_s, b_qbs, 0.125, f),
                        (wk_, b_wk_, kf, b_kf, qkvb, b_qkvb, 1.0, 8 + f)),
                 "v": ((wv_, b_wv_, vstage, None, qkvb, b_qkvb, 1.0, 16 + f),)}
        for (w_, b_w_, dst, b_dst, bias, b_bias, scale, bcol) in specs[which]:
            for tt in range(4):
                pb = tt % 2
                pv = cx.psum[pb].ap()[:, 0:512]
                bd = b_expst4[tt] if b_dst is None else b_dst
                for kc in range(ND):
                    p.do("tensor", lambda t, w_=w_, kc=kc, tt=tt, pv=pv: t.matmul(
                        pv, w_[:, kc * 128:(kc + 1) * 128], fma(ln3, kc, tt * 512, 512),
                        start=(kc == 0), stop=(kc == ND - 1)),
                        reads=[b_w_, b_ln3], writes=[cx.b_psum[pb]])
                p.do("scalar", lambda s, dst=dst, tt=tt, pv=pv, bias=bias, bcol=bcol,
                     scale=scale: s.activation(
                        out=dst[:, tt * 512:(tt + 1) * 512], in_=pv, func=AF.Identity,
                        bias=bias[:, bcol:bcol + 1], scale=scale),
                     reads=[cx.b_psum[pb], b_bias], writes=[bd])
        if which == "v":
            for tg in range(2):
                transposeN_bf(cx, [vstage[:, tc * 128:(tc + 1) * 128]
                                   for tc in range(tg * 8, tg * 8 + 8)],
                              [b_expst4[tc // 4] for tc in range(tg * 8, tg * 8 + 8)],
                              vt[:, tg * 1024:(tg + 1) * 1024], b_vt,
                              pb=tg % 2)

    load_pair_w(0, 0)
    l1_proj(0, "qk")
    l1_proj(0, "v")

    for f in range(8):
        if f < 7:
            load_pair_w(f + 1, (f + 1) % 2)
        qf, b_qf = qf2[f % 2], b_qf2[f % 2]
        kf, b_kf = kf2[f % 2], b_kf2[f % 2]
        vt, b_vt = vt2[f % 2], b_vt2[f % 2]
        gtmp = vt

        if globals().get("L1STAGE", 99) < 6:
            continue
        for g in range(4):
            rs8, b_rs8 = rs8g2[g % 2], b_rs8g2[g % 2]
            for i in range(4):
                qc = 4 * g + i
                expst, b_expst = expst4[i], b_expst4[i]
                for hh in range(2):
                    pbS = 2 * hh + i // 2
                    pvS = cx.psum[pbS].ap()
                    off = (i % 2) * 256
                    for cc in range(2):
                        if qc == 0 and cc == 0:
                            continue
                        kt = qc - 1 + cc
                        p.do("tensor", lambda t, hh=hh, cc=cc, kt=kt, qc=qc, pvS=pvS,
                             off=off, kf=kf, qf=qf: t.matmul(
                                pvS[:, off + cc * 128: off + cc * 128 + 128],
                                kf[64 * hh: 64 * hh + 64, kt * 128: kt * 128 + 128],
                                qf[64 * hh: 64 * hh + 64, qc * 128: qc * 128 + 128],
                                start=(cc == 0 or qc == 0), stop=(cc == 1)),
                            reads=[b_kf, b_qf], writes=[cx.b_psum[pbS]])
                    p.do("scalar", lambda s, pvS=pvS, off=off, hh=hh, expst=expst:
                         s.activation(out=expst[:, hh * 256: hh * 256 + 256],
                                      in_=pvS[:, off: off + 256], func=AF.Exp),
                         reads=[cx.b_psum[pbS]], writes=[b_expst])
                if qc == 0:
                    for hh_ in range(2):
                        p.do("vector", lambda v, expst=expst, hh_=hh_: v.memset(
                            expst[:, hh_ * 256: hh_ * 256 + 128], 0.0), writes=[b_expst])
                p.do("vector", lambda v, expst=expst: v.tensor_tensor(
                    out=expst.rearrange("p (h i) -> p h i", h=2),
                    in0=expst.rearrange("p (h i) -> p h i", h=2),
                    in1=locmask.unsqueeze(1).broadcast_to([128, 2, 256]), op=ALU.mult),
                    reads=[b_locm], writes=[b_expst])
            for i in range(4):
                qc = 4 * g + i
                pbP = 4 + i // 2
                off = (i % 2) * 256
                pvP = cx.psum[pbP].ap()
                expst, b_expst = expst4[i], b_expst4[i]
                first = True
                for hh in range(2):
                    for cc in range(2):
                        if qc == 0 and cc == 0:
                            continue
                        kt = qc - 1 + cc
                        p.do("tensor", lambda t, hh=hh, cc=cc, kt=kt, pvP=pvP, off=off,
                             first=first, expst=expst, vt=vt: t.matmul(
                                pvP[:, off + hh * 128: off + hh * 128 + 64],
                                expst[:, (hh * 2 + cc) * 128:(hh * 2 + cc) * 128 + 128],
                                vt[:, kt * 128 + hh * 64: kt * 128 + hh * 64 + 64],
                                start=first, stop=False),
                            reads=[b_expst, b_vt], writes=[cx.b_psum[pbP]])
                        p.do("tensor", lambda t, hh=hh, cc=cc, pvP=pvP, off=off,
                             expst=expst: t.matmul(
                                pvP[:, off + hh * 128 + 64: off + hh * 128 + 65],
                                expst[:, (hh * 2 + cc) * 128:(hh * 2 + cc) * 128 + 128],
                                cx.ones_col,
                                start=False, stop=(hh == 1 and cc == 1)),
                            reads=[b_expst, cx.b_ones_col], writes=[cx.b_psum[pbP]])
                        first = False
            for b in range(2):
                pvP = cx.psum[4 + b].ap()
                p.do("vector", lambda v, b=b, pvP=pvP, rs8=rs8: v.tensor_copy(
                    out=rs8[:, b * 4:(b + 1) * 4],
                    in_=pvP.rearrange("p (x c) -> p x c", c=128)[:, :, 64]),
                    reads=[cx.b_psum[4 + b]], writes=[b_rs8])
            p.do("vector", lambda v, rs8=rs8: v.tensor_scalar(
                out=rs8, in0=rs8, scalar1=1e-30, scalar2=None, op0=ALU.add),
                writes=[b_rs8])
            p.do("vector", lambda v, rs8=rs8: v.reciprocal(out=rs8, in_=rs8),
                 writes=[b_rs8])
            for b in range(2):
                pvP = cx.psum[4 + b].ap()
                qcA = 4 * g + 2 * b
                p.do("vector", lambda v, b=b, pvP=pvP, rs8=rs8, qcA=qcA: v.tensor_tensor(
                    out=locf[:, qcA * 128: qcA * 128 + 256].rearrange(
                        "p (x d) -> p x d", d=64),
                    in0=pvP.rearrange("p (x c) -> p x c", c=128)[:, :, 0:64],
                    in1=rs8[:, b * 4:(b + 1) * 4].unsqueeze(2).broadcast_to([128, 4, 64]),
                    op=ALU.mult),
                    reads=[cx.b_psum[4 + b], b_rs8], writes=[b_loc])

        if globals().get("L1STAGE", 99) < 7:
            continue
        if f < 7:
            l1_proj(f + 1, "qk")
        for tg in range(2):
            transposeN_bf(cx, [fma(rem, f, tc * 128, 128)
                               for tc in range(tg * 8, tg * 8 + 8)],
                          [b_rem] * 8, remt[:, tg * 1024:(tg + 1) * 1024], b_remt, pb=tg % 2)
        p.do("gpsimd", lambda v, gtmp=gtmp: v.tensor_tensor(
            out=gtmp.rearrange("p (q i) -> p q i", q=NT),
            in0=locf.rearrange("p (q i) -> p q i", q=NT),
            in1=glt.unsqueeze(1).broadcast_to([128, NT, 128]), op=ALU.mult),
            reads=[b_loc, b_glt], writes=[b_vt])
        p.do("vector", lambda v, gtmp=gtmp: v.reduce_sum(
            out=g1.rearrange("p (q h) -> p q h", q=NT),
            in_=gtmp.rearrange("p (q h d) -> p q h d", q=NT, h=2), axis=AX.X),
            reads=[b_vt], writes=[b_g1])
        p.do("gpsimd", lambda v, gtmp=gtmp: v.tensor_tensor(
            out=gtmp.rearrange("p (q i) -> p q i", q=NT),
            in0=remt.rearrange("p (q i) -> p q i", q=NT),
            in1=grt.unsqueeze(1).broadcast_to([128, NT, 128]), op=ALU.mult),
            reads=[b_remt, b_grt], writes=[b_vt])
        p.do("vector", lambda v, gtmp=gtmp: v.reduce_sum(
            out=gg.rearrange("p (q h) -> p q h", q=NT),
            in_=gtmp.rearrange("p (q h d) -> p q h d", q=NT, h=2), axis=AX.X),
            reads=[b_vt], writes=[b_gg])
        p.do("vector", lambda v: v.tensor_add(out=gg, in0=gg, in1=g1),
             reads=[b_g1], writes=[b_gg])
        p.do("scalar", lambda s: s.activation(out=gg, in_=gg, func=AF.Sigmoid),
             reads=[], writes=[b_gg])
        p.do("vector", lambda v: v.tensor_sub(out=locf[:, :], in0=locf[:, :], in1=remt[:, :]),
             reads=[b_remt], writes=[b_loc])
        p.do("gpsimd", lambda v: v.tensor_tensor(
            out=locf.rearrange("p (q h d) -> p q h d", q=NT, h=2),
            in0=locf.rearrange("p (q h d) -> p q h d", q=NT, h=2),
            in1=gg.rearrange("p (q h) -> p q h", q=NT).unsqueeze(3).broadcast_to(
                [128, NT, 2, 64]), op=ALU.mult),
            reads=[b_gg], writes=[b_loc])
        p.do("vector", lambda v: v.tensor_add(out=remt[:, :], in0=remt[:, :], in1=locf[:, :]),
             reads=[b_loc], writes=[b_remt])
        for tg in range(2):
            transposeN_bf(cx, [remt[:, tc * 128:(tc + 1) * 128]
                               for tc in range(tg * 8, tg * 8 + 8)],
                          [b_remt] * 8, fma(rem, f, tg * 1024, 1024), b_rem, pb=tg % 2)
        if f < 7:
            l1_proj(f + 1, "v")

    if globals().get("L1STAGE", 99) < 8:
        return
    if globals().get("DBG"):
        dmix = nc.dram_tensor("d_mix", [128, ND * S], BF16, kind="ExternalOutput").ap()
        p.do("sync", lambda s: s.dma_start(out=dmix, in_=rem), reads=[b_rem], dma=True)
        for nm, ap_, b_ in (("d_qf7", qf, b_qf), ("d_kf7", kf, b_kf),
                            ("d_vt7", vt, b_vt), ("d_loc7", locf, b_loc)):
            dt_ = nc.dram_tensor(nm, [128, S], BF16, kind="ExternalOutput").ap()
            p.do("sync", lambda s, dt_=dt_, ap_=ap_: s.dma_start(out=dt_, in_=ap_),
                 reads=[b_], dma=True)

    for tc in range(NT):
        for ih in range(2):
            pb = (tc * 2 + ih) % 4
            pv = cx.psum[pb].ap()[:, 0:512]
            for kc in range(ND):
                p.do("tensor", lambda t, tc=tc, ih=ih, kc=kc, pv=pv: t.matmul(
                    pv, fma(rem, kc, tc * 128, 128),
                    wout[:, kc * D + ih * 512: kc * D + ih * 512 + 512],
                    start=(kc == 0), stop=False),
                    reads=[b_rem, b_wout], writes=[cx.b_psum[pb]])
            p.do("tensor", lambda t, ih=ih, pv=pv: t.matmul(
                pv, cx.ones_row[:, 0:128], outb_row[:, ih * 512:(ih + 1) * 512],
                start=False, stop=True),
                reads=[cx.b_ones_row, b_outbr], writes=[cx.b_psum[pb]])
            p.do("vector", lambda v, tc=tc, ih=ih, pv=pv: v.tensor_add(
                out=cx.xt[tc][:, ih * 512:(ih + 1) * 512],
                in0=cx.xt[tc][:, ih * 512:(ih + 1) * 512], in1=pv),
                reads=[cx.b_psum[pb]], writes=[cx.bx[tc]])
            if ih == 1:
                ln_stats_tile(cx, tc)





def build_l1_ffn(cx, H):
    nc, p = cx.nc, cx.p
    p.fence()
    for a in (cx.BIG1, cx.BIG2, cx.STRIPE, cx.WT, cx.SCR):
        a.reset()
    Q = float(1.0 / 127.0)

    ln4 = cx.BIG1.take(BF16, 128, ND * S); b_ln4 = Buf("ln4")
    layernorm(cx, ln4, b_ln4, pre_stats=True)

    h1c = [cx.BIG2.take(BF16, 128, S) for _ in range(8)] + \
          [cx.STRIPE.take(BF16, 128, S) for _ in range(4)]
    b_h1 = [Buf(f"h1_{i}") for i in range(12)]

    wg = [cx.WT.take(BF16, 128, ND * 128) for _ in range(2)]
    wu = [cx.WT.take(BF16, 128, ND * 128) for _ in range(2)]
    wd = [cx.WT.take(BF16, 128, 1024) for _ in range(12)]
    b_wg = [Buf("l1f_wg0"), Buf("l1f_wg1")]
    b_wu = [Buf("l1f_wu0"), Buf("l1f_wu1")]
    b_wd = [Buf(f"l1f_wd{i}") for i in range(12)]
    sg = cx.SCR.take(BF16, 128, 1024); b_sg = Buf("l1f_sg")

    def load_gu(ic, slot):
        for (t_, b_, name) in ((wg[slot], b_wg[slot], "l1f_wgT"),
                               (wu[slot], b_wu[slot], "l1f_wuT")):
            src_ = H[name][:, ic * 128:(ic + 1) * 128].rearrange("(c p) i -> p c i", p=128)
            p.do("gpsimd", lambda g, t_=t_, src_=src_: g.dma_start(
                out=t_.rearrange("p (c i) -> p c i", i=128), in_=src_),
                writes=[b_], dma=True)

    def load_wd(ic, icl):
        src_ = H["l1f_wdT"][ic * 128:(ic + 1) * 128, :]
        p.do("gpsimd", lambda g, icl=icl, src_=src_: g.dma_start(
            out=wd[icl][:, :], in_=src_), writes=[b_wd[icl]], dma=True)

    load_gu(0, 0)
    for half in range(2):
        for icl in range(12):
            ic = half * 12 + icl
            slot = ic % 2
            if ic + 1 < 24:
                load_gu(ic + 1, 1 - slot)
            load_wd(ic, icl)
            for tt in range(4):
                pbg = tt % 2
                pbu = 2 + tt % 2
                pg = cx.psum[pbg].ap()[:, 0:512]
                pu = cx.psum[pbu].ap()[:, 0:512]
                for kc in range(ND):
                    p.do("tensor", lambda t, tt=tt, kc=kc, pg=pg, slot=slot: t.matmul(
                        pg, wg[slot][:, kc * 128:(kc + 1) * 128], fma(ln4, kc, tt * 512, 512),
                        start=(kc == 0), stop=(kc == ND - 1)),
                        reads=[b_wg[slot], b_ln4], writes=[cx.b_psum[pbg]])
                    p.do("tensor", lambda t, tt=tt, kc=kc, pu=pu, slot=slot: t.matmul(
                        pu, wu[slot][:, kc * 128:(kc + 1) * 128], fma(ln4, kc, tt * 512, 512),
                        start=(kc == 0), stop=(kc == ND - 1)),
                        reads=[b_wu[slot], b_ln4], writes=[cx.b_psum[pbu]])
                sgh = sg[:, (tt % 2) * 512:(tt % 2) * 512 + 512]
                p.do("scalar", lambda s, sgh=sgh, pg=pg: s.activation(
                    out=sgh, in_=pg, func=AF.Silu, scale=Q),
                    reads=[cx.b_psum[pbg]], writes=[b_sg])
                p.do("vector", lambda v, icl=icl, tt=tt, sgh=sgh, pu=pu: v.scalar_tensor_tensor(
                    out=h1c[icl][:, tt * 512:(tt + 1) * 512], in0=pu, scalar=Q, in1=sgh,
                    op0=ALU.mult, op1=ALU.mult),
                    reads=[cx.b_psum[pbu], b_sg], writes=[b_h1[icl]])
        for tc in range(NT):
            for dc in range(2):
                pbd = 4 + (tc * 2 + dc) % 2
                pd = cx.psum[pbd].ap()[:, 0:512]
                for icl in range(12):
                    p.do("tensor", lambda t, tc=tc, dc=dc, icl=icl, pd=pd: t.matmul(
                        pd, h1c[icl][:, tc * 128:(tc + 1) * 128],
                        wd[icl][:, dc * 512:(dc + 1) * 512],
                        start=(icl == 0), stop=(icl == 11)),
                        reads=[b_h1[icl], b_wd[icl]], writes=[cx.b_psum[pbd]])
                p.do("vector", lambda v, tc=tc, dc=dc, pd=pd: v.scalar_tensor_tensor(
                    out=cx.xt[tc][:, dc * 512:(dc + 1) * 512], in0=pd, scalar=Q,
                    in1=cx.xt[tc][:, dc * 512:(dc + 1) * 512], op0=ALU.mult, op1=ALU.add),
                    reads=[cx.b_psum[pbd]], writes=[cx.bx[tc]])



HBM_SPECS = {
    "L0_WqT": ([D, D], np.float32), "L0_WkT": ([D, D], np.float32),
    "L0_WvT": ([D, D], np.float32), "L0_WoT": ([D, D], np.float32),
    "L0_bq_c": ([128, 8], np.float32), "L0_bk_c": ([128, 8], np.float32),
    "L0_bv_c": ([128, 8], np.float32), "L0_bo_r": ([1, D], np.float32),
    "L0_projT": ([16, 64, 64], np.float32),
    "l0f_gwT": ([D, 8], np.float32), "l0f_lb": ([1, 8], np.float32),
    "l0f_egwT": ([8, D, 512], np.float32), "l0f_euwT": ([8, D, 512], np.float32),
    "l0f_edwT": ([8, 512, D], np.float32),
    "l0f_sgwT": ([D, D], np.float32), "l0f_suwT": ([D, D], np.float32),
    "l0f_sdwT": ([D, D], np.float32),
    "l1_qkvw": ([D, 3 * D], np.float32), "l1_qkvb_c": ([128, 24], np.float32),
    "l1_rqw": ([D, 256], np.float32), "l1_rqb_c": ([128, 2], np.float32),
    "l1_rkvw": ([D, 512], np.float32), "l1_rkvb_c": ([128, 4], np.float32),
    "l1_routw": ([256, D], np.float32), "l1_routb_c": ([128, 8], np.float32),
    "l1_outw": ([D, D], np.float32), "l1_outb_r": ([1, D], np.float32),
    "l1_gl_r": ([1, 128], np.float32), "l1_gr_r": ([1, 128], np.float32),
    "l1f_wgT": ([D, 3 * D], np.int32), "l1f_wuT": ([D, 3 * D], np.int32),
    "l1f_wdT": ([3 * D, D], np.int32),
}


def _prep_weights(inputs):
    c = np.ascontiguousarray
    H = {}
    H["L0_WqT"] = c(inputs["L0_attn_Wq"].T)
    H["L0_WkT"] = c(inputs["L0_attn_Wk"].T)
    H["L0_WvT"] = c(inputs["L0_attn_Wv"].T)
    H["L0_WoT"] = c(inputs["L0_attn_Wo"].T)
    H["L0_bq_c"] = c(inputs["L0_attn_bq"].reshape(8, 128).T)
    H["L0_bk_c"] = c(inputs["L0_attn_bk"].reshape(8, 128).T)
    H["L0_bv_c"] = c(inputs["L0_attn_bv"].reshape(8, 128).T)
    H["L0_bo_r"] = c(inputs["L0_attn_bo"].reshape(1, D))
    H["L0_projT"] = c(inputs["L0_attn_proj"].transpose(0, 2, 1))
    H["l0f_gwT"] = c(inputs["L0_ffn_gate_w"].T)
    H["l0f_lb"] = c(inputs["L0_ffn_lb_bias"].reshape(1, 8))
    H["l0f_egwT"] = c(inputs["L0_ffn_expert_gate_w"].transpose(0, 2, 1))
    H["l0f_euwT"] = c(inputs["L0_ffn_expert_up_w"].transpose(0, 2, 1))
    H["l0f_edwT"] = c(inputs["L0_ffn_expert_down_w"].transpose(0, 2, 1))
    H["l0f_sgwT"] = c(inputs["L0_ffn_shared_gate_w"].T)
    H["l0f_suwT"] = c(inputs["L0_ffn_shared_up_w"].T)
    H["l0f_sdwT"] = c(inputs["L0_ffn_shared_down_w"].T)
    H["l1_qkvw"] = c(inputs["L1_attn_qkv_w"])
    H["l1_qkvb_c"] = c(inputs["L1_attn_qkv_b"].reshape(24, 128).T)
    H["l1_rqw"] = c(inputs["L1_attn_rq_w"])
    H["l1_rqb_c"] = c(inputs["L1_attn_rq_b"].reshape(2, 128).T)
    H["l1_rkvw"] = c(inputs["L1_attn_rkv_w"])
    H["l1_rkvb_c"] = c(inputs["L1_attn_rkv_b"].reshape(4, 128).T)
    H["l1_routw"] = c(inputs["L1_attn_rout_w"])
    H["l1_routb_c"] = c(inputs["L1_attn_rout_b"].reshape(8, 128).T)
    H["l1_outw"] = c(inputs["L1_attn_out_w"])
    H["l1_outb_r"] = c(inputs["L1_attn_out_b"].reshape(1, D))
    H["l1_gl_r"] = c(np.tile(inputs["L1_attn_gl_w"].reshape(1, 64), (1, 2)))
    H["l1_gr_r"] = c(np.tile(inputs["L1_attn_gr_w"].reshape(1, 64), (1, 2)))
    H["l1f_wgT"] = c(inputs["L1_ffn_w_gate"].T)
    H["l1f_wuT"] = c(inputs["L1_ffn_w_up"].T)
    H["l1f_wdT"] = c(inputs["L1_ffn_w_down"].T)
    return H


BLOCKS = ("l0a", "l0f", "l1a", "l1f")
DBG = False


def _build_program(blocks):
    nc = bass.Bass("TRN2", target_bir_lowering=False)
    xin = nc.dram_tensor("x", [S, D], F32, kind="ExternalInput").ap()
    xout = nc.dram_tensor("xo", [S, D], F32, kind="ExternalOutput").ap()
    npdt = {np.float32: F32, np.int32: I32}
    Hb = {k: nc.dram_tensor(k, list(shape), npdt[dt], kind="ExternalInput").ap()
          for k, (shape, dt) in HBM_SPECS.items()}
    p = Prog(nc)
    globals()["LASTPROG"] = p
    cx = build_context(nc, p)
    load_x(cx, xin)
    if "l0a" in blocks:
        build_l0_attn(cx, Hb)
    if "l0f" in blocks:
        build_l0_ffn(cx, Hb)
    if "l1a" in blocks:
        build_l1_attn(cx, Hb)
    if "l1f" in blocks:
        build_l1_ffn(cx, Hb)
    final = store_x(cx, xout)
    p.finalize_and_emit(final)
    return nc


def kernel(**inputs):
    H = _prep_weights(inputs)
    x = np.ascontiguousarray(inputs["L0_attn_query"].astype(np.float32, copy=False))
    nc = _build_program(BLOCKS)
    in_maps = []
    for core in range(8):
        m = {"x": np.ascontiguousarray(x[core])}
        m.update(H)
        in_maps.append(m)
    res = run_bass_kernel_spmd(nc, in_maps, core_ids=list(range(8)))
    return np.stack([res.results[i]["xo"] for i in range(8)], axis=0)
